# version 08
# baseline (speedup 1.0000x reference)
import contextlib
import numpy as np
import ml_dtypes
import concourse.bass as bass
import concourse.mybir as mybir
from concourse.bass_utils import run_bass_kernel_spmd

F32 = mybir.dt.float32
I32 = mybir.dt.int32
BF16 = mybir.dt.bfloat16
AF = mybir.ActivationFunctionType
ALU = mybir.AluOpType
AX = mybir.AxisListType

P = 128
T = 2048
H = 2048
FI = 8192
NT = T // P
NJ = H // P
NW = FI // 512
NIC = FI // P
NH = 16
DH = 128
QSCALE = 1.0 / float(np.sqrt(DH))
EPS = 1e-5
NEGBIG = -1.0e30
ENGS = ("gpsimd", "sync", "tensor", "scalar", "vector")


class _DryInst:
    def then_inc(self, *a, **k):
        return self


_DRYINST = _DryInst()


class _DryEngine:
    def __getattr__(self, name):
        def f(*a, **k):
            return _DRYINST
        return f


_DRY = _DryEngine()


class Sem:
    __slots__ = ("h", "v", "name")

    def __init__(self, nc, ctx, name):
        self.h = ctx.enter_context(nc.semaphore(name))
        self.v = 0
        self.name = name

    def inc(self, inst, n=1):
        self.v += n
        if inst is not _DRYINST:
            inst.then_inc(self.h, n)
        return self.v

    def tok(self):
        return (self, self.v)


def wait_all(eng, toks):
    for (s, v) in toks:
        if v > 0:
            eng.wait_ge(s.h, v)


def build_nc(n_phases=3, attn_heads=NH):
    nc = bass.Bass("TRN2", target_bir_lowering=False)
    ap = lambda t: t.ap()

    x0 = ap(nc.dram_tensor("x0", [T, H], F32, kind="ExternalInput"))
    wgP = ap(nc.dram_tensor("wgP", [H, FI], F32, kind="ExternalInput"))
    wuP = ap(nc.dram_tensor("wuP", [H, FI], F32, kind="ExternalInput"))
    wdP = ap(nc.dram_tensor("wdP", [FI, H], F32, kind="ExternalInput"))
    wqT = ap(nc.dram_tensor("wqT", [H, H], F32, kind="ExternalInput"))
    wkT = ap(nc.dram_tensor("wkT", [H, H], F32, kind="ExternalInput"))
    wvT = ap(nc.dram_tensor("wvT", [H, H], F32, kind="ExternalInput"))
    woT = ap(nc.dram_tensor("woT", [H, H], F32, kind="ExternalInput"))
    y = ap(nc.dram_tensor("y", [T, H], F32, kind="ExternalOutput"))
    x1 = ap(nc.dram_tensor("x1", [T, H], F32))
    x2 = ap(nc.dram_tensor("x2", [T, H], F32))
    hT = ap(nc.dram_tensor("hT", [NT, P, NIC, P], BF16))
    atd = ap(nc.dram_tensor("atd", [NT, P, NH, P], BF16))

    ffn1_dst = y if n_phases == 1 else x1
    attn_dst = y if n_phases == 2 else x2

    ctx = contextlib.ExitStack()
    with ctx:
        E = ctx.enter_context
        ident = E(nc.sbuf_tensor("ident", [P, P], BF16))
        cmask32 = E(nc.sbuf_tensor("cmask32", [P, P], F32))
        cmask16 = E(nc.sbuf_tensor("cmask16", [P, P], BF16))
        nsum = E(nc.sbuf_tensor("nsum", [P, NT], F32))
        sumsq = E(nc.sbuf_tensor("sumsq", [P, NT], F32))
        negmean = E(nc.sbuf_tensor("negmean", [P, NT], F32))
        rstd = E(nc.sbuf_tensor("rstd", [P, NT], F32))
        tB = E(nc.sbuf_tensor("tB", [P, NT], F32))
        tC = E(nc.sbuf_tensor("tC", [P, NT], F32))
        tD = E(nc.sbuf_tensor("tD", [P, NT], F32))
        xta = [E(nc.sbuf_tensor(f"xta{i}", [P, T], F32)) for i in range(2)]
        epsrow = E(nc.sbuf_tensor("epsrow", [P, 1], F32))

        def mks(name):
            return Sem(nc, ctx, name)

        gp = mks("gp")
        dv0 = mks("dv0")
        block = E(nc.Block())

        def run_walks(fn, sems, scope_name):
            snap = [(s, s.v) for s in sems]
            with nc.named_scope(scope_name):
                for en in ENGS:
                    reg = getattr(block, en)

                    def body(eng, en=en):
                        for (s, v0) in snap:
                            s.v = v0
                        emap = {n: (_DRY if n != en else eng) for n in ENGS}
                        fn(emap)
                    reg(body)

        def walk_init(EM):
            g = EM["gpsimd"]
            v = EM["vector"]
            g.memset(ident[:], 0.0)
            gp.inc(g.affine_select(out=ident[:], in_=ident[:], compare_op=ALU.not_equal,
                                   fill=1.0, base=0, pattern=[[-1, P]], channel_multiplier=1))
            g.memset(cmask32[:], 0.0)
            gp.inc(g.affine_select(out=cmask32[:], in_=cmask32[:], compare_op=ALU.is_ge,
                                   fill=NEGBIG, base=0, pattern=[[-1, P]], channel_multiplier=1))
            gp.inc(g.memset(epsrow[:], EPS))
            v.wait_ge(gp.h, 3)
            dv0.inc(v.tensor_copy(cmask16[:], cmask32[:]))

        run_walks(walk_init, [gp, dv0], "init")
        tok0 = [gp.tok(), dv0.tok()]

        def make_ln(tag, x_src, xlnT, lnscr, xln16, ps_sq, ps_trA, ps_trB, prev):
            LX = mks(f"{tag}_lx")
            LV = mks(f"{tag}_lv")
            LS = mks(f"{tag}_ls")
            LP = mks(f"{tag}_lp")
            LE = mks(f"{tag}_le")

            def walk(EM):
                sy = EM["sync"]
                v = EM["vector"]
                sc = EM["scalar"]
                te = EM["tensor"]
                lv_stats, ls_stats = [], []
                x3 = xlnT[:].rearrange("p (j t2) -> p j t2", j=NJ)
                for t in range(NT):
                    if t == 0:
                        wait_all(sy, prev)
                    if t >= 2:
                        sy.wait_ge(LV.h, lv_stats[t - 2])
                        sy.wait_ge(LS.h, ls_stats[t - 2])
                    LX.inc(sy.dma_start(out=xta[t % 2][:], in_=x_src[t * P:(t + 1) * P, :]), 16)
                    if t == 0:
                        wait_all(v, prev)
                        wait_all(sc, prev)
                    v.wait_ge(LX.h, 16 * (t + 1))
                    lv_stats.append(LV.inc(v.reduce_sum(nsum[:, t:t + 1], xta[t % 2][:],
                                                        axis=AX.X, negate=True)))
                    sc.wait_ge(LX.h, 16 * (t + 1))
                    ls_stats.append(LS.inc(sc.activation(ps_sq[:], xta[t % 2][:], AF.Square,
                                                         accum_out=sumsq[:, t:t + 1])))
                v.wait_ge(LS.h, ls_stats[-1])
                v_nm = LV.inc(v.tensor_scalar(negmean[:], nsum[:], 1.0 / H, None, ALU.mult))
                sc.wait_ge(LV.h, v_nm)
                s_m2 = LS.inc(sc.activation(tD[:], negmean[:], AF.Square))
                v.wait_ge(LS.h, s_m2)
                v_var = LV.inc(v.scalar_tensor_tensor(tC[:], sumsq[:], 1.0 / H, tD[:],
                                                      ALU.mult, ALU.subtract))
                sc.wait_ge(LV.h, v_var)
                s_sqrt = LS.inc(sc.activation(tB[:], tC[:], AF.Sqrt, bias=epsrow[:, 0:1]))
                v.wait_ge(LS.h, s_sqrt)
                v_rstd = LV.inc(v.reciprocal(rstd[:], tB[:]))
                lv_apply, lp_tr, le_ev = [], [], []
                for t in range(NT):
                    if t < 2:
                        sy.wait_ge(LV.h, v_rstd)
                    else:
                        sy.wait_ge(LV.h, lv_apply[t - 2])
                    LX.inc(sy.dma_start(out=xta[t % 2][:], in_=x_src[t * P:(t + 1) * P, :]), 16)
                    s = t % 2
                    v.wait_ge(LX.h, 16 * (NT + t + 1))
                    if t == 0:
                        v.wait_ge(LV.h, v_rstd)
                    v.tensor_scalar(lnscr[s][:], xta[s][:], negmean[:, t:t + 1], None, ALU.add)
                    lv_apply.append(LV.inc(v.tensor_scalar(xln16[s][:], lnscr[s][:],
                                                           rstd[:, t:t + 1], None, ALU.mult)))
                    te.wait_ge(LV.h, lv_apply[t])
                    if t == 0:
                        te.wait_ge(gp.h, 1)
                    if t >= 1:
                        te.wait_ge(LE.h, le_ev[t - 1][0])
                    for j in range(8):
                        i1 = te.transpose(ps_trA[:, j * P:(j + 1) * P],
                                          xln16[s][:, j * P:(j + 1) * P], ident[:])
                    vA = LP.inc(i1)
                    if t >= 1:
                        te.wait_ge(LE.h, le_ev[t - 1][1])
                    for j in range(8):
                        i2 = te.transpose(ps_trB[:, j * P:(j + 1) * P],
                                          xln16[s][:, (8 + j) * P:(9 + j) * P], ident[:])
                    vB = LP.inc(i2)
                    lp_tr.append((vA, vB))
                    v.wait_ge(LP.h, lp_tr[t][0])
                    eA = LE.inc(v.tensor_copy(x3[:, 0:8, t * P:(t + 1) * P], ps_trA[:]))
                    v.wait_ge(LP.h, lp_tr[t][1])
                    eB = LE.inc(v.tensor_copy(x3[:, 8:16, t * P:(t + 1) * P], ps_trB[:]))
                    le_ev.append((eA, eB))

            return walk, {"LX": LX, "LV": LV, "LS": LS, "LP": LP, "LE": LE}

        def emit_ffn(tag, x_src, x_dst, prev):
            with contextlib.ExitStack() as sc1x:
                E1 = sc1x.enter_context
                xlnT = E1(nc.sbuf_tensor(f"{tag}_xlnT", [P, NJ * T], BF16))
                wg16 = [E1(nc.sbuf_tensor(f"{tag}_wg{i}", [P, NJ * 512], BF16)) for i in range(2)]
                wu16 = [E1(nc.sbuf_tensor(f"{tag}_wu{i}", [P, NJ * 512], BF16)) for i in range(2)]
                lnscr = [E1(nc.sbuf_tensor(f"{tag}_lnscr{i}", [P, T], F32)) for i in range(2)]
                xln16 = [E1(nc.sbuf_tensor(f"{tag}_xln16{i}", [P, T], BF16)) for i in range(2)]
                hbuf = [E1(nc.sbuf_tensor(f"{tag}_h{i}", [P, 512], F32)) for i in range(3)]
                sg = [E1(nc.sbuf_tensor(f"{tag}_sg{i}", [P, 512], F32)) for i in range(3)]
                habs = [E1(nc.sbuf_tensor(f"{tag}_habs{i}", [P, 512], F32)) for i in range(3)]
                trb = [E1(nc.sbuf_tensor(f"{tag}_tr{i}", [P, P], F32)) for i in range(7)]
                t2b = [E1(nc.sbuf_tensor(f"{tag}_t2{i}", [P, P], F32)) for i in range(3)]
                mkb = [E1(nc.sbuf_tensor(f"{tag}_mk{i}", [P, P], F32)) for i in range(3)]
                hsp = [E1(nc.sbuf_tensor(f"{tag}_hsp{i}", [P, 512], BF16)) for i in range(3)]
                hstg = [E1(nc.sbuf_tensor(f"{tag}_hstg{i}", [P, 512], BF16)) for i in range(3)]
                ps_trA = E1(nc.psum_tensor(f"{tag}_ptA", [P, 1024], BF16))
                ps_trB = E1(nc.psum_tensor(f"{tag}_ptB", [P, 1024], BF16))

                WL = mks(f"{tag}_wl")

                def wload(g, w, buf):
                    for (dst, src) in ((wg16[buf], wgP), (wu16[buf], wuP)):
                        WL.inc(g.dma_start(
                            out=dst[:].rearrange("p (j n) -> p j n", j=NJ),
                            in_=src[:, w * 512:(w + 1) * 512]
                            .rearrange("(j p) n -> j p n", p=P)
                            .rearrange("j p n -> p j n")), 16)

                with contextlib.ExitStack() as lnps:
                    ps_sq = lnps.enter_context(nc.psum_tensor(f"{tag}_pssq", [P, H], F32))
                    lnwalk, lnsems = make_ln(tag + "ln", x_src, xlnT, lnscr, xln16,
                                             ps_sq, ps_trA, ps_trB, prev)

                    def walk_pre(EM):
                        g = EM["gpsimd"]
                        wait_all(g, prev)
                        wload(g, 0, 0)
                        wload(g, 1, 1)
                    run_walks(walk_pre, [WL], tag + "pre")
                    run_walks(lnwalk, list(lnsems.values()), tag + "ln")
                LE = lnsems["LE"]
                le_final = LE.v

                ps_g = [E1(nc.psum_tensor(f"{tag}_psg{i}", [P, 512], F32)) for i in range(3)]
                ps_u = [E1(nc.psum_tensor(f"{tag}_psu{i}", [P, 512], F32)) for i in range(3)]

                PE1 = mks(f"{tag}_pe1")
                SC1 = mks(f"{tag}_sc1")
                DV1 = mks(f"{tag}_dv1")
                SH = mks(f"{tag}_sh")
                K = NW * NT

                def walk_s1(EM):
                    te = EM["tensor"]
                    sc = EM["scalar"]
                    v = EM["vector"]
                    sy = EM["sync"]
                    g = EM["gpsimd"]
                    x3 = xlnT[:].rearrange("p (j t2) -> p j t2", j=NJ)
                    mmg, trg = [], []
                    silu_v, sh_v = [], []
                    mult_v, apply_v, evac_v = [], [], []

                    def emit_tr(k):
                        s = k % 3
                        te.wait_ge(DV1.h, apply_v[k])
                        if k >= 2:
                            te.wait_ge(DV1.h, evac_v[k - 2])
                        pt = ps_trA if k % 2 == 0 else ps_trB
                        for c in range(4):
                            i1 = te.transpose(pt[:, c * P:(c + 1) * P],
                                              hsp[s][:, c * P:(c + 1) * P], ident[:])
                        trg.append(PE1.inc(i1))

                    def emit_ev_store(kk):
                        v.wait_ge(PE1.h, trg[kk])
                        if kk >= 3:
                            v.wait_ge(SH.h, sh_v[kk - 3])
                        pt = ps_trA if kk % 2 == 0 else ps_trB
                        evac_v.append(DV1.inc(v.tensor_copy(hstg[kk % 3][:], pt[:, 0:512])))
                        ww, tt = divmod(kk, NT)
                        sy.wait_ge(DV1.h, evac_v[kk])
                        sh_v.append(SH.inc(sy.dma_start(
                            out=hT[tt][:, 4 * ww:4 * (ww + 1), :],
                            in_=hstg[kk % 3][:].rearrange("p (a t2) -> p a t2", a=4)), 16))

                    for k in range(K):
                        w, t = divmod(k, NT)
                        s = k % 3
                        if k == 0:
                            te.wait_ge(LE.h, le_final)
                        te.wait_ge(WL.h, 32 * (w + 1))
                        if k >= 3:
                            te.wait_ge(SC1.h, silu_v[k - 3])
                            te.wait_ge(DV1.h, mult_v[k - 3])
                        wg3 = wg16[w % 2][:].rearrange("p (j n) -> p j n", j=NJ)
                        wu3 = wu16[w % 2][:].rearrange("p (j n) -> p j n", j=NJ)
                        for j in range(NJ):
                            lh = x3[:, j, t * P:(t + 1) * P]
                            te.matmul(ps_g[s][:], lh, wg3[:, j, :],
                                      start=(j == 0), stop=(j == NJ - 1))
                            i2 = te.matmul(ps_u[s][:], lh, wu3[:, j, :],
                                           start=(j == 0), stop=(j == NJ - 1))
                        mmg.append(PE1.inc(i2))
                        if k >= 1:
                            emit_tr(k - 1)

                        if t == NT - 1 and w + 2 < NW:
                            g.wait_ge(PE1.h, mmg[-1])
                            wload(g, w + 2, w % 2)

                        sc.wait_ge(PE1.h, mmg[k])
                        silu_v.append(SC1.inc(sc.activation(sg[s][:], ps_g[s][:], AF.Silu)))

                        v.wait_ge(SC1.h, silu_v[k])
                        mult_v.append(DV1.inc(v.tensor_tensor(hbuf[s][:], sg[s][:],
                                                              ps_u[s][:], ALU.mult)))
                        v.tensor_scalar(habs[s][:].bitcast(I32), hbuf[s][:].bitcast(I32),
                                        0x7FFFFFFF, None, ALU.bitwise_and)
                        q = habs[s][:].rearrange("p (r g) -> p r g", r=4)
                        hq = hbuf[s][:].rearrange("p (r g) -> p r g", r=4)
                        ho = hsp[s][:].rearrange("p (r g) -> p r g", r=4)
                        v.tensor_tensor(trb[0][:], q[:, 0, :], q[:, 1, :], ALU.max)
                        v.tensor_tensor(trb[1][:], q[:, 0, :], q[:, 1, :], ALU.min)
                        v.tensor_tensor(trb[2][:], q[:, 2, :], q[:, 3, :], ALU.max)
                        v.tensor_tensor(trb[3][:], q[:, 2, :], q[:, 3, :], ALU.min)
                        v.tensor_tensor(trb[4][:], trb[0][:], trb[2][:], ALU.min)
                        v.tensor_tensor(trb[5][:], trb[1][:], trb[3][:], ALU.max)
                        v.tensor_tensor(t2b[s][:], trb[4][:], trb[5][:], ALU.max)
                        last = None
                        for r in range(4):
                            v.tensor_tensor(mkb[s][:], q[:, r, :], t2b[s][:], ALU.is_ge)
                            last = v.tensor_tensor(ho[:, r, :], hq[:, r, :], mkb[s][:],
                                                   ALU.mult)
                        apply_v.append(DV1.inc(last))
                        if k >= 1:
                            emit_ev_store(k - 1)
                    emit_tr(K - 1)
                    emit_ev_store(K - 1)

                run_walks(walk_s1, [WL, PE1, SC1, DV1, SH], tag + "_s1")
                s1_toks = [(PE1, PE1.v), (DV1, DV1.v), (SC1, SC1.v), (SH, SH.v),
                           (WL, WL.v), (lnsems["LX"], lnsems["LX"].v),
                           (lnsems["LS"], lnsems["LS"].v)]

            with contextlib.ExitStack() as sc2x:
                E2 = sc2x.enter_context
                wd16 = [E2(nc.sbuf_tensor(f"{tag}_wd{i}", [P, NIC * 512], BF16)) for i in range(2)]
                hTt = [E2(nc.sbuf_tensor(f"{tag}_hTt{i}", [P, NIC * P], BF16)) for i in range(3)]
                xoq = [E2(nc.sbuf_tensor(f"{tag}_xoq{i}", [P, 512], F32)) for i in range(2)]
                xnq = [E2(nc.sbuf_tensor(f"{tag}_xnq{i}", [P, 512], F32)) for i in range(2)]
                ps_d = [E2(nc.psum_tensor(f"{tag}_psd{i}", [P, 512], F32)) for i in range(2)]

                W2 = mks(f"{tag}_w2")
                LH = mks(f"{tag}_lh")
                LX2 = mks(f"{tag}_lx2")
                PE2 = mks(f"{tag}_pe2")
                DV2 = mks(f"{tag}_dv2")
                SX = mks(f"{tag}_sx")

                def walk_s2(EM):
                    te = EM["tensor"]
                    v = EM["vector"]
                    sy = EM["sync"]
                    sq = EM["scalar"]
                    g = EM["gpsimd"]
                    w2_v, lh_v, lx2_v, pe2_v, dv2_v, sx_v = [], [], [], [], [], []

                    def wdload(Q):
                        w2_v.append(W2.inc(g.dma_start(
                            out=wd16[Q % 2][:].rearrange("p (a n) -> p a n", a=NIC),
                            in_=wdP[:, Q * 512:(Q + 1) * 512]
                            .rearrange("(a p) n -> a p n", p=P)
                            .rearrange("a p n -> p a n")), 16))

                    wait_all(g, s1_toks)
                    wdload(0)
                    wdload(1)
                    for Q in range(4):
                        for t in range(NT):
                            k = Q * NT + t
                            s = k % 2
                            sh = k % 3
                            if k == 0:
                                wait_all(sy, s1_toks)
                            if k >= 3:
                                sy.wait_ge(PE2.h, pe2_v[k - 3])
                            lh_v.append(LH.inc(sy.dma_start(
                                out=hTt[sh][:].rearrange("p (a t2) -> p a t2", a=NIC),
                                in_=hT[t]), 16))
                            if k == 0:
                                wait_all(sq, s1_toks)
                            if k >= 2:
                                sq.wait_ge(DV2.h, dv2_v[k - 2])
                            lx2_v.append(LX2.inc(sq.dma_start(
                                out=xoq[s][:],
                                in_=x_src[t * P:(t + 1) * P, Q * 512:(Q + 1) * 512]), 16))

                            if k == 0:
                                wait_all(te, s1_toks)
                            te.wait_ge(W2.h, w2_v[Q])
                            te.wait_ge(LH.h, lh_v[k])
                            if k >= 2:
                                te.wait_ge(DV2.h, dv2_v[k - 2])
                            wd3 = wd16[Q % 2][:].rearrange("p (a n) -> p a n", a=NIC)
                            hT3 = hTt[sh][:].rearrange("p (a t2) -> p a t2", a=NIC)
                            for ic in range(NIC):
                                i1 = te.matmul(ps_d[s][:], hT3[:, ic, :], wd3[:, ic, :],
                                               start=(ic == 0), stop=(ic == NIC - 1))
                            pe2_v.append(PE2.inc(i1))

                            if t == NT - 1 and Q + 2 < 4:
                                g.wait_ge(PE2.h, pe2_v[-1])
                                wdload(Q + 2)

                            v.wait_ge(PE2.h, pe2_v[k])
                            v.wait_ge(LX2.h, lx2_v[k])
                            if k >= 2:
                                v.wait_ge(SX.h, sx_v[k - 2])
                            dv2_v.append(DV2.inc(v.scalar_tensor_tensor(
                                xnq[s][:], ps_d[s][:], 0.5, xoq[s][:], ALU.mult, ALU.add)))

                            sq.wait_ge(DV2.h, dv2_v[k])
                            sx_v.append(SX.inc(sq.dma_start(
                                out=x_dst[t * P:(t + 1) * P, Q * 512:(Q + 1) * 512],
                                in_=xnq[s][:]), 16))

                run_walks(walk_s2, [W2, LH, LX2, PE2, DV2, SX], tag + "_s2")
                toks_out = [(PE2, PE2.v), (DV2, DV2.v), (SX, SX.v), (W2, W2.v),
                            (LH, LH.v), (LX2, LX2.v)]
            return toks_out

        def emit_attn(tag, x_src, x_dst, prev):
            with contextlib.ExitStack() as scax:
                Ea = scax.enter_context
                xlnT = Ea(nc.sbuf_tensor(f"{tag}_xlnT", [P, NJ * T], BF16))
                wq16 = [Ea(nc.sbuf_tensor(f"{tag}_wq{i}", [P, NJ * P], BF16)) for i in range(2)]
                wk16 = [Ea(nc.sbuf_tensor(f"{tag}_wk{i}", [P, NJ * P], BF16)) for i in range(2)]
                wv16 = [Ea(nc.sbuf_tensor(f"{tag}_wv{i}", [P, NJ * P], BF16)) for i in range(2)]
                Qh = [Ea(nc.sbuf_tensor(f"{tag}_Qh{i}", [P, T], BF16)) for i in range(2)]
                Kh = [Ea(nc.sbuf_tensor(f"{tag}_Kh{i}", [P, T], BF16)) for i in range(2)]
                Vtm = [Ea(nc.sbuf_tensor(f"{tag}_Vtm{i}", [P, T], BF16)) for i in range(2)]
                Vfm = Ea(nc.sbuf_tensor(f"{tag}_Vfm", [P, T], BF16))
                E1b = [Ea(nc.sbuf_tensor(f"{tag}_E1{i}", [P, T], BF16)) for i in range(2)]
                ETg = [Ea(nc.sbuf_tensor(f"{tag}_ET{i}", [P, 16 * 256], BF16)) for i in range(4)]
                den = [Ea(nc.sbuf_tensor(f"{tag}_den{i}", [P, P], F32)) for i in range(2)]
                denb = Ea(nc.sbuf_tensor(f"{tag}_denb", [P, P], F32))
                rdenb = Ea(nc.sbuf_tensor(f"{tag}_rdenb", [P, P], F32))
                nbx = Ea(nc.sbuf_tensor(f"{tag}_nbx", [P, 4], F32))
                ssb = [Ea(nc.sbuf_tensor(f"{tag}_ssb{i}", [P, T], F32)) for i in range(2)]
                E2b = [ssb[i][:].bitcast(BF16)[:, 0:T] for i in range(2)]
                avstg = [Ea(nc.sbuf_tensor(f"{tag}_avs{i}", [P, 256], BF16)) for i in range(2)]
                lnscr = ssb
                xln16 = [Ea(nc.sbuf_tensor(f"{tag}_xln16{i}", [P, T], BF16)) for i in range(2)]
                ps_s = Ea(nc.psum_tensor(f"{tag}_pss", [P, H], F32))
                ps_q = [Ea(nc.psum_tensor(f"{tag}_psq{i}", [P, 512], F32)) for i in range(2)]
                ps_tr = Ea(nc.psum_tensor(f"{tag}_pstr", [P, 1024], BF16))

                AW = mks(f"{tag}_aw")

                def awload(g, h, buf):
                    for (dst, src) in ((wq16[buf], wqT), (wk16[buf], wkT), (wv16[buf], wvT)):
                        AW.inc(g.dma_start(
                            out=dst[:].rearrange("p (j n) -> p j n", j=NJ),
                            in_=src[:, h * P:(h + 1) * P]
                            .rearrange("(j p) n -> j p n", p=P)
                            .rearrange("j p n -> p j n")), 16)

                with contextlib.ExitStack() as lnps:
                    ps_trB2 = lnps.enter_context(nc.psum_tensor(f"{tag}_ptB2", [P, 1024], BF16))
                    lnwalk, lnsems = make_ln(tag + "ln", x_src, xlnT, lnscr, xln16,
                                             ps_s, ps_tr, ps_trB2, prev)

                    def walk_pre(EM):
                        g = EM["gpsimd"]
                        wait_all(g, prev)
                        awload(g, 0, 0)
                        awload(g, 1, 1)
                    run_walks(walk_pre, [AW], tag + "pre")
                    run_walks(lnwalk, list(lnsems.values()), tag + "ln")
                LE = lnsems["LE"]
                le_final = LE.v

                ps_av = Ea(nc.psum_tensor(f"{tag}_psav", [P, 512], F32))

                PEa = mks(f"{tag}_pea")
                SCa = mks(f"{tag}_sca")
                DVa = mks(f"{tag}_dva")
                STa = mks(f"{tag}_sta")

                def walk_heads(EM):
                    te = EM["tensor"]
                    sc = EM["scalar"]
                    v = EM["vector"]
                    sy = EM["sync"]
                    g = EM["gpsimd"]
                    x3 = xlnT[:].rearrange("p (j t2) -> p j t2", j=NJ)
                    aw_v = [AW.v - 48, AW.v]
                    psq_hist = []
                    qkv_ready = {}
                    exp_v, e2_v, etg_v, scp_v = {}, {}, {}, {}
                    av_pe, av_ev, sta_v = {}, {}, {}
                    mz_v = {}
                    trfree_v = [0]
                    last_scp = [0]

                    def emit_qkv(h):
                        hs = h % 2
                        for (widx, wbuf, is_q, is_v) in ((0, wq16[hs], True, False),
                                                         (1, wk16[hs], False, False),
                                                         (2, wv16[hs], False, True)):
                            w3 = wbuf[:].rearrange("p (j n) -> p j n", j=NJ)
                            for tc in range(4):
                                ci = len(psq_hist)
                                if h == 0 and widx == 0 and tc == 0:
                                    te.wait_ge(LE.h, le_final)
                                te.wait_ge(AW.h, aw_v[h])
                                if ci >= 2:
                                    kind, val = psq_hist[ci - 2]
                                    te.wait_ge(SCa.h if kind == "s" else DVa.h, val)
                                for j in range(NJ):
                                    i1 = te.matmul(ps_q[tc % 2][:], w3[:, j, :],
                                                   x3[:, j, tc * 512:(tc + 1) * 512],
                                                   start=(j == 0), stop=(j == NJ - 1))
                                pev = PEa.inc(i1)
                                if is_q:
                                    sc.wait_ge(PEa.h, pev)
                                    psq_hist.append(("s", SCa.inc(sc.activation(
                                        Qh[hs][:, tc * 512:(tc + 1) * 512],
                                        ps_q[tc % 2][:], AF.Copy, scale=QSCALE))))
                                else:
                                    tgt = Vfm if is_v else Kh[hs]
                                    v.wait_ge(PEa.h, pev)
                                    psq_hist.append(("v", DVa.inc(v.tensor_copy(
                                        tgt[:, tc * 512:(tc + 1) * 512], ps_q[tc % 2][:]))))
                        vfm_done = psq_hist[-1][1]
                        for b in range(2):
                            te.wait_ge(DVa.h, max(vfm_done, trfree_v[0]) if b == 0
                                       else DVa.v)
                            for j in range(8):
                                i1 = te.transpose(ps_tr[:, j * P:(j + 1) * P],
                                                  Vfm[:, (b * 8 + j) * P:(b * 8 + j + 1) * P],
                                                  ident[:])
                            pev = PEa.inc(i1)
                            v.wait_ge(PEa.h, pev)
                            trfree_v[0] = DVa.inc(v.tensor_copy(
                                Vtm[hs][:, b * 1024:(b + 1) * 1024], ps_tr[:]))
                        qkv_ready[h] = (SCa.v, DVa.v)

                    def emit_trev(h, qt):
                        hs = h % 2
                        g2 = qt // 2
                        ql2 = qt % 2
                        gb = hs * 2 + g2 % 2
                        mz_dv = mz_v.get((h, g2), 0)
                        nbt = qt + 1
                        trpe = None
                        for b in range((nbt + 7) // 8):
                            j0 = b * 8
                            j1 = min(nbt, j0 + 8)
                            te.wait_ge(DVa.h, e2_v[(h, qt)])
                            te.wait_ge(DVa.h, trfree_v[0])
                            for j in range(j0, j1):
                                i1 = te.transpose(
                                    ps_tr[:, (j - j0) * P:(j - j0 + 1) * P],
                                    E2b[hs][:, j * P:(j + 1) * P], ident[:])
                            trpe = pev = PEa.inc(i1)
                            v.wait_ge(PEa.h, pev)
                            if b == 0 and mz_dv > 0:
                                v.wait_ge(DVa.h, mz_dv)
                            et3 = ETg[gb][:].rearrange("p (a n) -> p a n", a=16)
                            trfree_v[0] = DVa.inc(v.tensor_copy(
                                et3[:, j0:j1, ql2 * P:(ql2 + 1) * P],
                                ps_tr[:, 0:(j1 - j0) * P]))
                        etg_v[(h, qt)] = DVa.v
                        trpe_v[(h, qt)] = trpe

                    def emit_av(h, qt):
                        hs = h % 2
                        g2 = qt // 2
                        gb = hs * 2 + g2 % 2
                        te.wait_ge(DVa.h, etg_v[(h, qt)])
                        if av_free[0] > 0:
                            te.wait_ge(DVa.h, av_free[0])
                        et3 = ETg[gb][:].rearrange("p (a n) -> p a n", a=16)
                        nkc = 2 * (g2 + 1)
                        for kc in range(nkc):
                            i1 = te.matmul(ps_av[:, 0:256],
                                           Vtm[hs][:, kc * P:(kc + 1) * P],
                                           et3[:, kc, :],
                                           start=(kc == 0), stop=(kc == nkc - 1))
                        av_pe[(h, g2)] = PEa.inc(i1)
                        v.wait_ge(PEa.h, av_pe[(h, g2)])
                        skey = (h, g2 - 1)
                        if skey in sta_v:
                            v.wait_ge(STa.h, sta_v[skey])
                        av_free[0] = av_ev[(h, g2)] = DVa.inc(v.tensor_copy(
                            avstg[hs][:], ps_av[:, 0:256]))
                        sy.wait_ge(DVa.h, av_ev[(h, g2)])
                        sta_v[(h, g2)] = STa.inc(sy.dma_start(
                            out=atd[2 * g2:2 * (g2 + 1), :, h, :]
                            .rearrange("a d q -> d a q"),
                            in_=avstg[hs][:]
                            .rearrange("p (a q) -> p a q", a=2)), 16)

                    def chain_part1(h, qt):
                        hs = h % 2
                        L = (qt + 1) * P
                        g2 = qt // 2
                        ql2 = qt % 2
                        gb = hs * 2 + g2 % 2
                        if qt == 0:
                            scv, dvv = qkv_ready[h]
                            te.wait_ge(SCa.h, scv)
                            te.wait_ge(DVa.h, dvv)
                        if last_scp[0] > 0:
                            te.wait_ge(DVa.h, last_scp[0])
                        nc512 = (L + 511) // 512
                        for c in range(nc512):
                            n0 = c * 512
                            n1 = min(L, n0 + 512)
                            lastc = (c == nc512 - 1)
                            te.matmul(ps_s[:, n0:n1],
                                      Qh[hs][:, qt * P:(qt + 1) * P],
                                      Kh[hs][:, n0:n1], start=True, stop=not lastc)
                            if lastc:
                                i1 = te.matmul(ps_s[:, qt * P:L], ident[:], cmask16[:],
                                               start=False, stop=True)
                        sc_pe = PEa.inc(i1)
                        if qt > 0:
                            emit_trev(h, qt - 1)
                            if qt >= 2 and qt % 2 == 0:
                                emit_av(h, qt - 1)
                        v.wait_ge(PEa.h, sc_pe)
                        key = (h, qt - 1)
                        if key in exp_v:
                            v.wait_ge(SCa.h, exp_v[key])
                        if key in trpe_v and trpe_v[key] is not None:
                            v.wait_ge(PEa.h, trpe_v[key])
                        last_scp[0] = DVa.inc(v.tensor_copy(
                            ssb[hs][:, 0:L], ps_s[:, 0:L]))
                        scp_v[(h, qt)] = last_scp[0]
                        mx_dv = DVa.inc(v.reduce_max(nbx[:, hs:hs + 1],
                                                     ssb[hs][:, 0:L],
                                                     axis=AX.X, negate=True))
                        mx_v[(h, qt)] = mx_dv
                        if ql2 == 0 and qt < NT - 1:
                            mkey = (h, g2 - 2) if g2 >= 2 else (h - 2, g2 + 6)
                            if mkey in av_pe:
                                v.wait_ge(PEa.h, av_pe[mkey])
                            et3 = ETg[gb][:].rearrange("p (a n) -> p a n", a=16)
                            mz_v[(h, g2)] = DVa.inc(v.memset(
                                et3[:, qt + 1, 0:P], 0.0))

                    def chain_part2(h, qt):
                        hs = h % 2
                        L = (qt + 1) * P
                        sc.wait_ge(DVa.h, mx_v[(h, qt)])
                        exp_v[(h, qt)] = SCa.inc(sc.activation(
                            E1b[hs][:, 0:L], ssb[hs][:, 0:L], AF.Exp,
                            bias=nbx[:, hs:hs + 1],
                            accum_out=den[hs][:, qt:qt + 1]))
                        v.wait_ge(SCa.h, exp_v[(h, qt)])
                        if (h, qt - 1) in trpe_v and trpe_v[(h, qt - 1)] is not None:
                            v.wait_ge(PEa.h, trpe_v[(h, qt - 1)])
                        v.tensor_scalar(denb[:], den[hs][:], 1e-8, None, ALU.add)
                        v.reciprocal(rdenb[:], denb[:])
                        e2_v[(h, qt)] = DVa.inc(v.tensor_scalar(
                            E2b[hs][:, 0:L], E1b[hs][:, 0:L], rdenb[:, qt:qt + 1],
                            None, ALU.mult))

                    trpe_v = {}
                    mx_v = {}
                    av_free = [0]
                    emit_qkv(0)
                    emit_qkv(1)
                    npair = attn_heads // 2
                    for pr in range(npair):
                        hA, hB = 2 * pr, 2 * pr + 1
                        if pr + 1 < npair:
                            g.wait_ge(PEa.h, PEa.v)
                            awload(g, hA + 2, 0)
                            aw_v.append(AW.v - 48)
                            awload(g, hB + 2, 1)
                            aw_v.append(AW.v)
                            aw_v[hA + 2] = AW.v - 48
                            aw_v[hB + 2] = AW.v
                        for qt in range(NT):
                            chain_part1(hA, qt)
                            chain_part1(hB, qt)
                            chain_part2(hA, qt)
                            chain_part2(hB, qt)
                        emit_trev(hA, NT - 1)
                        emit_av(hA, NT - 1)
                        emit_trev(hB, NT - 1)
                        emit_av(hB, NT - 1)
                        if pr + 1 < npair:
                            emit_qkv(hA + 2)
                            emit_qkv(hB + 2)

                run_walks(walk_heads, [AW, PEa, SCa, DVa, STa], "heads")
                heads_toks = [(PEa, PEa.v), (DVa, DVa.v), (SCa, SCa.v), (STa, STa.v),
                              (AW, AW.v), (lnsems["LX"], lnsems["LX"].v)]

            with contextlib.ExitStack() as scox:
                Eo = scox.enter_context
                woT16 = Eo(nc.sbuf_tensor(f"{tag}_woT", [P, NJ * H], BF16))
                at_sb = [Eo(nc.sbuf_tensor(f"{tag}_at{i}", [P, NH * P], BF16)) for i in range(3)]
                xnw = [Eo(nc.sbuf_tensor(f"{tag}_xnw{i}", [P, T], F32)) for i in range(2)]
                ps_o = [Eo(nc.psum_tensor(f"{tag}_pso{i}", [P, H], F32)) for i in range(2)]

                WO = mks(f"{tag}_wo")
                LA = mks(f"{tag}_la")
                LXo = mks(f"{tag}_lxo")
                PEo = mks(f"{tag}_peo")
                DVo = mks(f"{tag}_dvo")
                SXo = mks(f"{tag}_sxo")

                def walk_o(EM):
                    te = EM["tensor"]
                    v = EM["vector"]
                    sy = EM["sync"]
                    sq = EM["scalar"]
                    g = EM["gpsimd"]
                    la_v, lxo_v, peo_v, dvo_v, sxo_v = [], [], [], [], []
                    wait_all(g, heads_toks)
                    wo3 = woT16[:].rearrange("p (j n) -> p j n", j=NJ)
                    for hhalf in range(2):
                        WO.inc(g.dma_start(
                            out=wo3[:, hhalf * 8:(hhalf + 1) * 8, :],
                            in_=woT[hhalf * 1024:(hhalf + 1) * 1024, :]
                            .rearrange("(j p) n -> j p n", p=P)
                            .rearrange("j p n -> p j n")), 16)
                    wo_done = WO.v
                    for qt in range(NT):
                        s = qt % 2
                        sa = qt % 3
                        if qt == 0:
                            wait_all(sy, heads_toks)
                        if qt >= 3:
                            sy.wait_ge(PEo.h, peo_v[qt - 3])
                        la_v.append(LA.inc(sy.dma_start(
                            out=at_sb[sa][:].rearrange("p (a n) -> p a n", a=NH),
                            in_=atd[qt]), 16))
                        if qt == 0:
                            wait_all(sq, heads_toks)
                        if qt >= 2:
                            sq.wait_ge(DVo.h, dvo_v[qt - 2])
                        lxo_v.append(LXo.inc(sq.dma_start(
                            out=xta[s][:], in_=x_src[qt * P:(qt + 1) * P, :]), 16))

                        if qt == 0:
                            wait_all(te, heads_toks)
                        te.wait_ge(WO.h, wo_done)
                        te.wait_ge(LA.h, la_v[qt])
                        if qt >= 2:
                            te.wait_ge(DVo.h, dvo_v[qt - 2])
                        for j in range(NJ):
                            for cc in range(4):
                                i1 = te.matmul(ps_o[s][:, cc * 512:(cc + 1) * 512],
                                               at_sb[sa][:, j * P:(j + 1) * P],
                                               wo3[:, j, cc * 512:(cc + 1) * 512],
                                               start=(j == 0), stop=(j == NJ - 1))
                        peo_v.append(PEo.inc(i1))

                        v.wait_ge(PEo.h, peo_v[qt])
                        v.wait_ge(LXo.h, lxo_v[qt])
                        if qt >= 2:
                            v.wait_ge(SXo.h, sxo_v[qt - 2])
                        dvo_v.append(DVo.inc(v.tensor_tensor(xnw[s][:], ps_o[s][:],
                                                             xta[s][:], ALU.add)))
                        sq.wait_ge(DVo.h, dvo_v[qt])
                        sxo_v.append(SXo.inc(sq.dma_start(
                            out=x_dst[qt * P:(qt + 1) * P, :], in_=xnw[s][:]), 16))

                run_walks(walk_o, [WO, LA, LXo, PEo, DVo, SXo], "oproj")
                toks_out = [(PEo, PEo.v), (DVo, DVo.v), (SXo, SXo.v), (WO, WO.v),
                            (LA, LA.v), (LXo, LXo.v)]
            return toks_out

        last = emit_ffn("f1", x0, ffn1_dst, tok0)
        if n_phases >= 2:
            last = emit_attn("at", x1, attn_dst, last)
        if n_phases >= 3:
            last = emit_ffn("f2", x2, y, last)

        def _fin(sy):
            wait_all(sy, last)
        block.sync(_fin)

    return nc


def _perm():
    idx = np.empty(FI, dtype=np.int64)
    p = 0
    for w in range(NW):
        for r in range(4):
            for g in range(128):
                idx[p] = 4 * (128 * w + g) + r
                p += 1
    return idx


def kernel(L0_attn_x, L0_attn_wq, L0_attn_wk, L0_attn_wv, L0_attn_wo,
           L0_ffn_w_gate, L0_ffn_w_up, L0_ffn_w_down):
    perm = _perm()
    wgPh = np.ascontiguousarray(np.asarray(L0_ffn_w_gate)[:, perm])
    wuPh = np.ascontiguousarray(np.asarray(L0_ffn_w_up)[:, perm])
    wdPh = np.ascontiguousarray(np.asarray(L0_ffn_w_down)[perm, :])
    wqTh = np.ascontiguousarray(np.asarray(L0_attn_wq).T)
    wkTh = np.ascontiguousarray(np.asarray(L0_attn_wk).T)
    wvTh = np.ascontiguousarray(np.asarray(L0_attn_wv).T)
    woTh = np.ascontiguousarray(np.asarray(L0_attn_wo).T)
    x = np.ascontiguousarray(np.asarray(L0_attn_x), dtype=np.float32)

    nc = build_nc(3)
    in_maps = []
    for b in range(8):
        in_maps.append({
            "x0": x[b], "wgP": wgPh, "wuP": wuPh, "wdP": wdPh,
            "wqT": wqTh, "wkT": wkTh, "wvT": wvTh, "woT": woTh,
        })
    res = run_bass_kernel_spmd(nc, in_maps, core_ids=list(range(8)))
    out = np.stack([np.asarray(r["y"]) for r in res.results], axis=0)
    return out.astype(np.float32)


# version 10
# speedup vs baseline: 1.0271x; 1.0271x over previous
import contextlib
import numpy as np
import ml_dtypes
import concourse.bass as bass
import concourse.mybir as mybir
from concourse.bass_utils import run_bass_kernel_spmd

F32 = mybir.dt.float32
I32 = mybir.dt.int32
BF16 = mybir.dt.bfloat16
AF = mybir.ActivationFunctionType
ALU = mybir.AluOpType
AX = mybir.AxisListType

P = 128
T = 2048
H = 2048
FI = 8192
NT = T // P
NJ = H // P
NW = FI // 512
NIC = FI // P
NH = 16
DH = 128
QSCALE = 1.0 / float(np.sqrt(DH))
EPS = 1e-5
NEGBIG = -1.0e30
ENGS = ("gpsimd", "sync", "tensor", "scalar", "vector")


class _DryInst:
    def then_inc(self, *a, **k):
        return self


_DRYINST = _DryInst()


class _DryEngine:
    def __getattr__(self, name):
        def f(*a, **k):
            return _DRYINST
        return f


_DRY = _DryEngine()


class Sem:
    __slots__ = ("h", "v", "name")

    def __init__(self, nc, ctx, name):
        self.h = ctx.enter_context(nc.semaphore(name))
        self.v = 0
        self.name = name

    def inc(self, inst, n=1):
        self.v += n
        if inst is not _DRYINST:
            inst.then_inc(self.h, n)
        return self.v

    def tok(self):
        return (self, self.v)


def wait_all(eng, toks):
    for (s, v) in toks:
        if v > 0:
            eng.wait_ge(s.h, v)


def build_nc(n_phases=3, attn_heads=NH):
    nc = bass.Bass("TRN2", target_bir_lowering=False)
    ap = lambda t: t.ap()

    x0 = ap(nc.dram_tensor("x0", [T, H], F32, kind="ExternalInput"))
    wgP = ap(nc.dram_tensor("wgP", [H, FI], F32, kind="ExternalInput"))
    wuP = ap(nc.dram_tensor("wuP", [H, FI], F32, kind="ExternalInput"))
    wdP = ap(nc.dram_tensor("wdP", [FI, H], F32, kind="ExternalInput"))
    wqT = ap(nc.dram_tensor("wqT", [H, H], F32, kind="ExternalInput"))
    wkT = ap(nc.dram_tensor("wkT", [H, H], F32, kind="ExternalInput"))
    wvT = ap(nc.dram_tensor("wvT", [H, H], F32, kind="ExternalInput"))
    woT = ap(nc.dram_tensor("woT", [H, H], F32, kind="ExternalInput"))
    y = ap(nc.dram_tensor("y", [T, H], F32, kind="ExternalOutput"))
    x1 = ap(nc.dram_tensor("x1", [T, H], F32))
    x2 = ap(nc.dram_tensor("x2", [T, H], F32))
    hT = ap(nc.dram_tensor("hT", [NT, P, NIC, P], BF16))
    atd = ap(nc.dram_tensor("atd", [NT, P, NH, P], BF16))

    ffn1_dst = y if n_phases == 1 else x1
    attn_dst = y if n_phases == 2 else x2

    ctx = contextlib.ExitStack()
    with ctx:
        E = ctx.enter_context
        ident = E(nc.sbuf_tensor("ident", [P, P], BF16))
        cmask32 = E(nc.sbuf_tensor("cmask32", [P, P], F32))
        cmask16 = E(nc.sbuf_tensor("cmask16", [P, P], BF16))
        nsum = E(nc.sbuf_tensor("nsum", [P, NT], F32))
        sumsq = E(nc.sbuf_tensor("sumsq", [P, NT], F32))
        negmean = E(nc.sbuf_tensor("negmean", [P, NT], F32))
        rstd = E(nc.sbuf_tensor("rstd", [P, NT], F32))
        tB = E(nc.sbuf_tensor("tB", [P, NT], F32))
        tC = E(nc.sbuf_tensor("tC", [P, NT], F32))
        tD = E(nc.sbuf_tensor("tD", [P, NT], F32))
        xta = [E(nc.sbuf_tensor(f"xta{i}", [P, T], F32)) for i in range(2)]
        epsrow = E(nc.sbuf_tensor("epsrow", [P, 1], F32))

        def mks(name):
            return Sem(nc, ctx, name)

        gp = mks("gp")
        dv0 = mks("dv0")
        block = E(nc.Block())

        def run_walks(fn, sems, scope_name):
            snap = [(s, s.v) for s in sems]
            with nc.named_scope(scope_name):
                for en in ENGS:
                    reg = getattr(block, en)

                    def body(eng, en=en):
                        for (s, v0) in snap:
                            s.v = v0
                        emap = {n: (_DRY if n != en else eng) for n in ENGS}
                        fn(emap)
                    reg(body)

        def walk_init(EM):
            g = EM["gpsimd"]
            v = EM["vector"]
            g.memset(ident[:], 0.0)
            gp.inc(g.affine_select(out=ident[:], in_=ident[:], compare_op=ALU.not_equal,
                                   fill=1.0, base=0, pattern=[[-1, P]], channel_multiplier=1))
            g.memset(cmask32[:], 0.0)
            gp.inc(g.affine_select(out=cmask32[:], in_=cmask32[:], compare_op=ALU.is_ge,
                                   fill=NEGBIG, base=0, pattern=[[-1, P]], channel_multiplier=1))
            gp.inc(g.memset(epsrow[:], EPS))
            v.wait_ge(gp.h, 3)
            dv0.inc(v.tensor_copy(cmask16[:], cmask32[:]))

        run_walks(walk_init, [gp, dv0], "init")
        tok0 = [gp.tok(), dv0.tok()]

        def make_ln(tag, x_src, xlnT, lnscr, xln16, ps_sq, ps_trA, ps_trB, prev):
            LX = mks(f"{tag}_lx")
            LV = mks(f"{tag}_lv")
            LS = mks(f"{tag}_ls")
            LP = mks(f"{tag}_lp")
            LE = mks(f"{tag}_le")

            def walk(EM):
                sy = EM["sync"]
                v = EM["vector"]
                sc = EM["scalar"]
                te = EM["tensor"]
                lv_stats, ls_stats = [], []
                x3 = xlnT[:].rearrange("p (j t2) -> p j t2", j=NJ)
                for t in range(NT):
                    if t == 0:
                        wait_all(sy, prev)
                    if t >= 2:
                        sy.wait_ge(LV.h, lv_stats[t - 2])
                        sy.wait_ge(LS.h, ls_stats[t - 2])
                    LX.inc(sy.dma_start(out=xta[t % 2][:], in_=x_src[t * P:(t + 1) * P, :]), 16)
                    if t == 0:
                        wait_all(v, prev)
                        wait_all(sc, prev)
                    v.wait_ge(LX.h, 16 * (t + 1))
                    lv_stats.append(LV.inc(v.reduce_sum(nsum[:, t:t + 1], xta[t % 2][:],
                                                        axis=AX.X, negate=True)))
                    sc.wait_ge(LX.h, 16 * (t + 1))
                    ls_stats.append(LS.inc(sc.activation(ps_sq[:], xta[t % 2][:], AF.Square,
                                                         accum_out=sumsq[:, t:t + 1])))
                v.wait_ge(LS.h, ls_stats[-1])
                v_nm = LV.inc(v.tensor_scalar(negmean[:], nsum[:], 1.0 / H, None, ALU.mult))
                sc.wait_ge(LV.h, v_nm)
                s_m2 = LS.inc(sc.activation(tD[:], negmean[:], AF.Square))
                v.wait_ge(LS.h, s_m2)
                v_var = LV.inc(v.scalar_tensor_tensor(tC[:], sumsq[:], 1.0 / H, tD[:],
                                                      ALU.mult, ALU.subtract))
                sc.wait_ge(LV.h, v_var)
                s_sqrt = LS.inc(sc.activation(tB[:], tC[:], AF.Sqrt, bias=epsrow[:, 0:1]))
                v.wait_ge(LS.h, s_sqrt)
                v_rstd = LV.inc(v.reciprocal(rstd[:], tB[:]))
                lv_apply, lp_tr, le_ev = [], [], []
                for t in range(NT):
                    if t < 2:
                        sy.wait_ge(LV.h, v_rstd)
                    else:
                        sy.wait_ge(LV.h, lv_apply[t - 2])
                    LX.inc(sy.dma_start(out=xta[t % 2][:], in_=x_src[t * P:(t + 1) * P, :]), 16)
                    s = t % 2
                    v.wait_ge(LX.h, 16 * (NT + t + 1))
                    if t == 0:
                        v.wait_ge(LV.h, v_rstd)
                    v.tensor_scalar(lnscr[s][:], xta[s][:], negmean[:, t:t + 1], None, ALU.add)
                    lv_apply.append(LV.inc(v.tensor_scalar(xln16[s][:], lnscr[s][:],
                                                           rstd[:, t:t + 1], None, ALU.mult)))
                    te.wait_ge(LV.h, lv_apply[t])
                    if t == 0:
                        te.wait_ge(gp.h, 1)
                    if t >= 1:
                        te.wait_ge(LE.h, le_ev[t - 1][0])
                    for j in range(8):
                        i1 = te.transpose(ps_trA[:, j * P:(j + 1) * P],
                                          xln16[s][:, j * P:(j + 1) * P], ident[:])
                    vA = LP.inc(i1)
                    if t >= 1:
                        te.wait_ge(LE.h, le_ev[t - 1][1])
                    for j in range(8):
                        i2 = te.transpose(ps_trB[:, j * P:(j + 1) * P],
                                          xln16[s][:, (8 + j) * P:(9 + j) * P], ident[:])
                    vB = LP.inc(i2)
                    lp_tr.append((vA, vB))
                    v.wait_ge(LP.h, lp_tr[t][0])
                    eA = LE.inc(v.tensor_copy(x3[:, 0:8, t * P:(t + 1) * P], ps_trA[:]))
                    v.wait_ge(LP.h, lp_tr[t][1])
                    eB = LE.inc(v.tensor_copy(x3[:, 8:16, t * P:(t + 1) * P], ps_trB[:]))
                    le_ev.append((eA, eB))

            return walk, {"LX": LX, "LV": LV, "LS": LS, "LP": LP, "LE": LE}

        def emit_ffn(tag, x_src, x_dst, prev):
            with contextlib.ExitStack() as sc1x:
                E1 = sc1x.enter_context
                xlnT = E1(nc.sbuf_tensor(f"{tag}_xlnT", [P, NJ * T], BF16))
                wg16 = [E1(nc.sbuf_tensor(f"{tag}_wg{i}", [P, NJ * 512], BF16)) for i in range(2)]
                wu16 = [E1(nc.sbuf_tensor(f"{tag}_wu{i}", [P, NJ * 512], BF16)) for i in range(2)]
                lnscr = [E1(nc.sbuf_tensor(f"{tag}_lnscr{i}", [P, T], F32)) for i in range(2)]
                xln16 = [E1(nc.sbuf_tensor(f"{tag}_xln16{i}", [P, T], BF16)) for i in range(2)]
                hbuf = [E1(nc.sbuf_tensor(f"{tag}_h{i}", [P, 512], F32)) for i in range(3)]
                sg = [E1(nc.sbuf_tensor(f"{tag}_sg{i}", [P, 512], F32)) for i in range(3)]
                habs = [E1(nc.sbuf_tensor(f"{tag}_habs{i}", [P, 512], F32)) for i in range(3)]
                trb = [E1(nc.sbuf_tensor(f"{tag}_tr{i}", [P, P], F32)) for i in range(7)]
                t2b = [E1(nc.sbuf_tensor(f"{tag}_t2{i}", [P, P], F32)) for i in range(3)]
                mkb = [E1(nc.sbuf_tensor(f"{tag}_mk{i}", [P, P], F32)) for i in range(3)]
                hsp = [E1(nc.sbuf_tensor(f"{tag}_hsp{i}", [P, 512], BF16)) for i in range(3)]
                hstg = [E1(nc.sbuf_tensor(f"{tag}_hstg{i}", [P, 512], BF16)) for i in range(3)]
                ps_trA = E1(nc.psum_tensor(f"{tag}_ptA", [P, 1024], BF16))
                ps_trB = E1(nc.psum_tensor(f"{tag}_ptB", [P, 1024], BF16))

                WL = mks(f"{tag}_wl")

                def wload(g, w, buf):
                    for (dst, src) in ((wg16[buf], wgP), (wu16[buf], wuP)):
                        WL.inc(g.dma_start(
                            out=dst[:].rearrange("p (j n) -> p j n", j=NJ),
                            in_=src[:, w * 512:(w + 1) * 512]
                            .rearrange("(j p) n -> j p n", p=P)
                            .rearrange("j p n -> p j n")), 16)

                with contextlib.ExitStack() as lnps:
                    ps_sq = lnps.enter_context(nc.psum_tensor(f"{tag}_pssq", [P, H], F32))
                    lnwalk, lnsems = make_ln(tag + "ln", x_src, xlnT, lnscr, xln16,
                                             ps_sq, ps_trA, ps_trB, prev)

                    def walk_pre(EM):
                        g = EM["gpsimd"]
                        wait_all(g, prev)
                        wload(g, 0, 0)
                        wload(g, 1, 1)
                    run_walks(walk_pre, [WL], tag + "pre")
                    run_walks(lnwalk, list(lnsems.values()), tag + "ln")
                LE = lnsems["LE"]
                le_final = LE.v

                ps_g = [E1(nc.psum_tensor(f"{tag}_psg{i}", [P, 512], F32)) for i in range(3)]
                ps_u = [E1(nc.psum_tensor(f"{tag}_psu{i}", [P, 512], F32)) for i in range(3)]

                PE1 = mks(f"{tag}_pe1")
                SC1 = mks(f"{tag}_sc1")
                DV1 = mks(f"{tag}_dv1")
                SH = mks(f"{tag}_sh")
                K = NW * NT

                def walk_s1(EM):
                    te = EM["tensor"]
                    sc = EM["scalar"]
                    v = EM["vector"]
                    sy = EM["sync"]
                    g = EM["gpsimd"]
                    x3 = xlnT[:].rearrange("p (j t2) -> p j t2", j=NJ)
                    mmg, trg = [], []
                    silu_v, sh_v = [], []
                    mult_v, apply_v, evac_v = [], [], []

                    def emit_tr(k):
                        s = k % 3
                        te.wait_ge(DV1.h, apply_v[k])
                        if k >= 2:
                            te.wait_ge(DV1.h, evac_v[k - 2])
                        pt = ps_trA if k % 2 == 0 else ps_trB
                        for c in range(4):
                            i1 = te.transpose(pt[:, c * P:(c + 1) * P],
                                              hsp[s][:, c * P:(c + 1) * P], ident[:])
                        trg.append(PE1.inc(i1))

                    def emit_ev_store(kk):
                        v.wait_ge(PE1.h, trg[kk])
                        if kk >= 3:
                            v.wait_ge(SH.h, sh_v[kk - 3])
                        pt = ps_trA if kk % 2 == 0 else ps_trB
                        evac_v.append(DV1.inc(v.tensor_copy(hstg[kk % 3][:], pt[:, 0:512])))
                        ww, tt = divmod(kk, NT)
                        sy.wait_ge(DV1.h, evac_v[kk])
                        sh_v.append(SH.inc(sy.dma_start(
                            out=hT[tt][:, 4 * ww:4 * (ww + 1), :],
                            in_=hstg[kk % 3][:].rearrange("p (a t2) -> p a t2", a=4)), 16))

                    for k in range(K):
                        w, t = divmod(k, NT)
                        s = k % 3
                        if k == 0:
                            te.wait_ge(LE.h, le_final)
                        te.wait_ge(WL.h, 32 * (w + 1))
                        if k >= 3:
                            te.wait_ge(SC1.h, silu_v[k - 3])
                            te.wait_ge(DV1.h, mult_v[k - 3])
                        wg3 = wg16[w % 2][:].rearrange("p (j n) -> p j n", j=NJ)
                        wu3 = wu16[w % 2][:].rearrange("p (j n) -> p j n", j=NJ)
                        for j in range(NJ):
                            lh = x3[:, j, t * P:(t + 1) * P]
                            te.matmul(ps_g[s][:], lh, wg3[:, j, :],
                                      start=(j == 0), stop=(j == NJ - 1))
                            i2 = te.matmul(ps_u[s][:], lh, wu3[:, j, :],
                                           start=(j == 0), stop=(j == NJ - 1))
                        mmg.append(PE1.inc(i2))
                        if k >= 1:
                            emit_tr(k - 1)

                        if t == NT - 1 and w + 2 < NW:
                            g.wait_ge(PE1.h, mmg[-1])
                            wload(g, w + 2, w % 2)

                        sc.wait_ge(PE1.h, mmg[k])
                        silu_v.append(SC1.inc(sc.activation(sg[s][:], ps_g[s][:], AF.Silu)))

                        v.wait_ge(SC1.h, silu_v[k])
                        mult_v.append(DV1.inc(v.tensor_tensor(hbuf[s][:], sg[s][:],
                                                              ps_u[s][:], ALU.mult)))
                        v.tensor_scalar(habs[s][:].bitcast(I32), hbuf[s][:].bitcast(I32),
                                        0x7FFFFFFF, None, ALU.bitwise_and)
                        q = habs[s][:].rearrange("p (r g) -> p r g", r=4)
                        hq = hbuf[s][:].rearrange("p (r g) -> p r g", r=4)
                        ho = hsp[s][:].rearrange("p (r g) -> p r g", r=4)
                        v.tensor_tensor(trb[0][:], q[:, 0, :], q[:, 1, :], ALU.max)
                        v.tensor_tensor(trb[1][:], q[:, 0, :], q[:, 1, :], ALU.min)
                        v.tensor_tensor(trb[2][:], q[:, 2, :], q[:, 3, :], ALU.max)
                        v.tensor_tensor(trb[3][:], q[:, 2, :], q[:, 3, :], ALU.min)
                        v.tensor_tensor(trb[4][:], trb[0][:], trb[2][:], ALU.min)
                        v.tensor_tensor(trb[5][:], trb[1][:], trb[3][:], ALU.max)
                        v.tensor_tensor(t2b[s][:], trb[4][:], trb[5][:], ALU.max)
                        last = None
                        for r in range(4):
                            v.tensor_tensor(mkb[s][:], q[:, r, :], t2b[s][:], ALU.is_ge)
                            last = v.tensor_tensor(ho[:, r, :], hq[:, r, :], mkb[s][:],
                                                   ALU.mult)
                        apply_v.append(DV1.inc(last))
                        if k >= 1:
                            emit_ev_store(k - 1)
                    emit_tr(K - 1)
                    emit_ev_store(K - 1)

                run_walks(walk_s1, [WL, PE1, SC1, DV1, SH], tag + "_s1")
                s1_toks = [(PE1, PE1.v), (DV1, DV1.v), (SC1, SC1.v), (SH, SH.v),
                           (WL, WL.v), (lnsems["LX"], lnsems["LX"].v),
                           (lnsems["LS"], lnsems["LS"].v)]

            with contextlib.ExitStack() as sc2x:
                E2 = sc2x.enter_context
                wd16 = [E2(nc.sbuf_tensor(f"{tag}_wd{i}", [P, NIC * 512], BF16)) for i in range(2)]
                hTt = [E2(nc.sbuf_tensor(f"{tag}_hTt{i}", [P, NIC * P], BF16)) for i in range(3)]
                xoq = [E2(nc.sbuf_tensor(f"{tag}_xoq{i}", [P, 512], F32)) for i in range(2)]
                xnq = [E2(nc.sbuf_tensor(f"{tag}_xnq{i}", [P, 512], F32)) for i in range(2)]
                ps_d = [E2(nc.psum_tensor(f"{tag}_psd{i}", [P, 512], F32)) for i in range(2)]

                W2 = mks(f"{tag}_w2")
                LH = mks(f"{tag}_lh")
                LX2 = mks(f"{tag}_lx2")
                PE2 = mks(f"{tag}_pe2")
                DV2 = mks(f"{tag}_dv2")
                SX = mks(f"{tag}_sx")

                def walk_s2(EM):
                    te = EM["tensor"]
                    v = EM["vector"]
                    sy = EM["sync"]
                    sq = EM["scalar"]
                    g = EM["gpsimd"]
                    w2_v, lh_v, lx2_v, pe2_v, dv2_v, sx_v = [], [], [], [], [], []

                    def wdload(Q):
                        w2_v.append(W2.inc(g.dma_start(
                            out=wd16[Q % 2][:].rearrange("p (a n) -> p a n", a=NIC),
                            in_=wdP[:, Q * 512:(Q + 1) * 512]
                            .rearrange("(a p) n -> a p n", p=P)
                            .rearrange("a p n -> p a n")), 16))

                    wait_all(g, s1_toks)
                    wdload(0)
                    wdload(1)
                    for Q in range(4):
                        for t in range(NT):
                            k = Q * NT + t
                            s = k % 2
                            sh = k % 3
                            if k == 0:
                                wait_all(sy, s1_toks)
                            if k >= 3:
                                sy.wait_ge(PE2.h, pe2_v[k - 3])
                            lh_v.append(LH.inc(sy.dma_start(
                                out=hTt[sh][:].rearrange("p (a t2) -> p a t2", a=NIC),
                                in_=hT[t]), 16))
                            if k == 0:
                                wait_all(sq, s1_toks)
                            if k >= 2:
                                sq.wait_ge(DV2.h, dv2_v[k - 2])
                            lx2_v.append(LX2.inc(sq.dma_start(
                                out=xoq[s][:],
                                in_=x_src[t * P:(t + 1) * P, Q * 512:(Q + 1) * 512]), 16))

                            if k == 0:
                                wait_all(te, s1_toks)
                            te.wait_ge(W2.h, w2_v[Q])
                            te.wait_ge(LH.h, lh_v[k])
                            if k >= 2:
                                te.wait_ge(DV2.h, dv2_v[k - 2])
                            wd3 = wd16[Q % 2][:].rearrange("p (a n) -> p a n", a=NIC)
                            hT3 = hTt[sh][:].rearrange("p (a t2) -> p a t2", a=NIC)
                            for ic in range(NIC):
                                i1 = te.matmul(ps_d[s][:], hT3[:, ic, :], wd3[:, ic, :],
                                               start=(ic == 0), stop=(ic == NIC - 1))
                            pe2_v.append(PE2.inc(i1))

                            if t == NT - 1 and Q + 2 < 4:
                                g.wait_ge(PE2.h, pe2_v[-1])
                                wdload(Q + 2)

                            v.wait_ge(PE2.h, pe2_v[k])
                            v.wait_ge(LX2.h, lx2_v[k])
                            if k >= 2:
                                v.wait_ge(SX.h, sx_v[k - 2])
                            dv2_v.append(DV2.inc(v.scalar_tensor_tensor(
                                xnq[s][:], ps_d[s][:], 0.5, xoq[s][:], ALU.mult, ALU.add)))

                            sq.wait_ge(DV2.h, dv2_v[k])
                            sx_v.append(SX.inc(sq.dma_start(
                                out=x_dst[t * P:(t + 1) * P, Q * 512:(Q + 1) * 512],
                                in_=xnq[s][:]), 16))

                run_walks(walk_s2, [W2, LH, LX2, PE2, DV2, SX], tag + "_s2")
                toks_out = [(PE2, PE2.v), (DV2, DV2.v), (SX, SX.v), (W2, W2.v),
                            (LH, LH.v), (LX2, LX2.v)]
            return toks_out

        def emit_attn(tag, x_src, x_dst, prev):
            with contextlib.ExitStack() as scax:
                Ea = scax.enter_context
                xlnT = Ea(nc.sbuf_tensor(f"{tag}_xlnT", [P, NJ * T], BF16))
                wq16 = [Ea(nc.sbuf_tensor(f"{tag}_wq{i}", [P, NJ * P], BF16)) for i in range(2)]
                wk16 = [Ea(nc.sbuf_tensor(f"{tag}_wk{i}", [P, NJ * P], BF16)) for i in range(2)]
                wv16 = [Ea(nc.sbuf_tensor(f"{tag}_wv{i}", [P, NJ * P], BF16)) for i in range(2)]
                Qh = [Ea(nc.sbuf_tensor(f"{tag}_Qh{i}", [P, T], BF16)) for i in range(2)]
                Kh = [Ea(nc.sbuf_tensor(f"{tag}_Kh{i}", [P, T], BF16)) for i in range(2)]
                Vtm = [Ea(nc.sbuf_tensor(f"{tag}_Vtm{i}", [P, T], BF16)) for i in range(2)]
                Qh.append(xta[0][:].bitcast(BF16)[:, 0:T])
                Kh.append(xta[0][:].bitcast(BF16)[:, T:2 * T])
                Vtm.append(xta[1][:].bitcast(BF16)[:, 0:T])
                Vfm = Ea(nc.sbuf_tensor(f"{tag}_Vfm", [P, T], BF16))
                E1b = [Ea(nc.sbuf_tensor(f"{tag}_E1{i}", [P, T], BF16)) for i in range(2)]
                ETg = [Ea(nc.sbuf_tensor(f"{tag}_ET{i}", [P, 16 * 256], BF16)) for i in range(4)]
                den = [Ea(nc.sbuf_tensor(f"{tag}_den{i}", [P, P], F32)) for i in range(2)]
                denb = Ea(nc.sbuf_tensor(f"{tag}_denb", [P, P], F32))
                rdenb = Ea(nc.sbuf_tensor(f"{tag}_rdenb", [P, P], F32))
                nbx = Ea(nc.sbuf_tensor(f"{tag}_nbx", [P, 4], F32))
                ssb = [Ea(nc.sbuf_tensor(f"{tag}_ssb{i}", [P, T], F32)) for i in range(2)]
                E2b = [ssb[i][:].bitcast(BF16)[:, 0:T] for i in range(2)]
                avstg = [Ea(nc.sbuf_tensor(f"{tag}_avs{i}", [P, 256], BF16)) for i in range(2)]
                lnscr = ssb
                xln16 = [Ea(nc.sbuf_tensor(f"{tag}_xln16{i}", [P, T], BF16)) for i in range(2)]
                ps_s = Ea(nc.psum_tensor(f"{tag}_pss", [P, H], F32))
                ps_q = [Ea(nc.psum_tensor(f"{tag}_psq{i}", [P, 512], F32)) for i in range(2)]
                ps_tr = Ea(nc.psum_tensor(f"{tag}_pstr", [P, 1024], BF16))

                AW = mks(f"{tag}_aw")

                def awload(g, h, buf):
                    for (dst, src) in ((wq16[buf], wqT), (wk16[buf], wkT), (wv16[buf], wvT)):
                        AW.inc(g.dma_start(
                            out=dst[:].rearrange("p (j n) -> p j n", j=NJ),
                            in_=src[:, h * P:(h + 1) * P]
                            .rearrange("(j p) n -> j p n", p=P)
                            .rearrange("j p n -> p j n")), 16)

                with contextlib.ExitStack() as lnps:
                    ps_trB2 = lnps.enter_context(nc.psum_tensor(f"{tag}_ptB2", [P, 1024], BF16))
                    lnwalk, lnsems = make_ln(tag + "ln", x_src, xlnT, lnscr, xln16,
                                             ps_s, ps_tr, ps_trB2, prev)

                    def walk_pre(EM):
                        g = EM["gpsimd"]
                        wait_all(g, prev)
                        awload(g, 0, 0)
                        awload(g, 1, 1)
                    run_walks(walk_pre, [AW], tag + "pre")
                    run_walks(lnwalk, list(lnsems.values()), tag + "ln")
                LE = lnsems["LE"]
                le_final = LE.v

                ps_av = Ea(nc.psum_tensor(f"{tag}_psav", [P, 512], F32))

                PEa = mks(f"{tag}_pea")
                SCa = mks(f"{tag}_sca")
                DVa = mks(f"{tag}_dva")
                STa = mks(f"{tag}_sta")

                def walk_heads(EM):
                    te = EM["tensor"]
                    sc = EM["scalar"]
                    v = EM["vector"]
                    sy = EM["sync"]
                    g = EM["gpsimd"]
                    x3 = xlnT[:].rearrange("p (j t2) -> p j t2", j=NJ)
                    aw_v = [AW.v - 48, AW.v]
                    psq_hist = []
                    qkv_ready = {}
                    exp_v, e2_v, etg_v, scp_v = {}, {}, {}, {}
                    av_pe, av_ev, sta_v = {}, {}, {}
                    mz_v = {}
                    trfree_v = [0]
                    last_scp = [0]

                    def qkv_units(h):
                        hq = h % 3
                        units = []
                        for (widx, is_q, is_v) in ((0, True, False), (1, False, False),
                                                   (2, False, True)):
                            for tc in range(4):
                                def u(widx=widx, tc=tc, is_q=is_q, is_v=is_v, h=h, hq=hq):
                                    wbuf = (wq16, wk16, wv16)[widx][h % 2]
                                    w3 = wbuf[:].rearrange("p (j n) -> p j n", j=NJ)
                                    ci = len(psq_hist)
                                    if h == 0 and widx == 0 and tc == 0:
                                        te.wait_ge(LE.h, le_final)
                                    te.wait_ge(AW.h, aw_v[h])
                                    if ci >= 2:
                                        kind, val = psq_hist[ci - 2]
                                        te.wait_ge(SCa.h if kind == "s" else DVa.h, val)
                                    for j in range(NJ):
                                        i1 = te.matmul(ps_q[tc % 2][:], w3[:, j, :],
                                                       x3[:, j, tc * 512:(tc + 1) * 512],
                                                       start=(j == 0), stop=(j == NJ - 1))
                                    pev = PEa.inc(i1)
                                    if is_q:
                                        sc.wait_ge(PEa.h, pev)
                                        psq_hist.append(("s", SCa.inc(sc.activation(
                                            Qh[hq][:, tc * 512:(tc + 1) * 512],
                                            ps_q[tc % 2][:], AF.Copy, scale=QSCALE))))
                                    else:
                                        tgt = Vfm if is_v else Kh[hq]
                                        v.wait_ge(PEa.h, pev)
                                        psq_hist.append(("v", DVa.inc(v.tensor_copy(
                                            tgt[:, tc * 512:(tc + 1) * 512],
                                            ps_q[tc % 2][:]))))
                                units.append(u)
                        for b in range(2):
                            def u(b=b, h=h, hq=hq):
                                vfm_done = psq_hist[-1][1]
                                te.wait_ge(DVa.h, max(vfm_done, trfree_v[0]) if b == 0
                                           else DVa.v)
                                for j in range(8):
                                    i1 = te.transpose(
                                        ps_tr[:, j * P:(j + 1) * P],
                                        Vfm[:, (b * 8 + j) * P:(b * 8 + j + 1) * P],
                                        ident[:])
                                pev = PEa.inc(i1)
                                v.wait_ge(PEa.h, pev)
                                trfree_v[0] = DVa.inc(v.tensor_copy(
                                    Vtm[hq][:, b * 1024:(b + 1) * 1024], ps_tr[:]))
                                if b == 1:
                                    qkv_ready[h] = (SCa.v, DVa.v)
                            units.append(u)
                        return units

                    def emit_qkv(h):
                        for u in qkv_units(h):
                            u()

                    def emit_trev(h, qt):
                        hs = h % 2
                        g2 = qt // 2
                        ql2 = qt % 2
                        gb = hs * 2 + g2 % 2
                        mz_dv = mz_v.get((h, g2), 0)
                        nbt = qt + 1
                        trpe = None
                        for b in range((nbt + 7) // 8):
                            j0 = b * 8
                            j1 = min(nbt, j0 + 8)
                            te.wait_ge(DVa.h, e2_v[(h, qt)])
                            te.wait_ge(DVa.h, trfree_v[0])
                            for j in range(j0, j1):
                                i1 = te.transpose(
                                    ps_tr[:, (j - j0) * P:(j - j0 + 1) * P],
                                    E2b[hs][:, j * P:(j + 1) * P], ident[:])
                            trpe = pev = PEa.inc(i1)
                            v.wait_ge(PEa.h, pev)
                            if b == 0 and mz_dv > 0:
                                v.wait_ge(DVa.h, mz_dv)
                            et3 = ETg[gb][:].rearrange("p (a n) -> p a n", a=16)
                            trfree_v[0] = DVa.inc(v.tensor_copy(
                                et3[:, j0:j1, ql2 * P:(ql2 + 1) * P],
                                ps_tr[:, 0:(j1 - j0) * P]))
                        etg_v[(h, qt)] = DVa.v
                        trpe_v[(h, qt)] = trpe

                    def emit_av(h, qt):
                        hs = h % 2
                        g2 = qt // 2
                        gb = hs * 2 + g2 % 2
                        te.wait_ge(DVa.h, etg_v[(h, qt)])
                        if av_free[0] > 0:
                            te.wait_ge(DVa.h, av_free[0])
                        et3 = ETg[gb][:].rearrange("p (a n) -> p a n", a=16)
                        nkc = 2 * (g2 + 1)
                        hq = h % 3
                        for kc in range(nkc):
                            i1 = te.matmul(ps_av[:, 0:256],
                                           Vtm[hq][:, kc * P:(kc + 1) * P],
                                           et3[:, kc, :],
                                           start=(kc == 0), stop=(kc == nkc - 1))
                        av_pe[(h, g2)] = PEa.inc(i1)
                        v.wait_ge(PEa.h, av_pe[(h, g2)])
                        skey = (h, g2 - 1)
                        if skey in sta_v:
                            v.wait_ge(STa.h, sta_v[skey])
                        av_free[0] = av_ev[(h, g2)] = DVa.inc(v.tensor_copy(
                            avstg[hs][:], ps_av[:, 0:256]))
                        sy.wait_ge(DVa.h, av_ev[(h, g2)])
                        sta_v[(h, g2)] = STa.inc(sy.dma_start(
                            out=atd[2 * g2:2 * (g2 + 1), :, h, :]
                            .rearrange("a d q -> d a q"),
                            in_=avstg[hs][:]
                            .rearrange("p (a q) -> p a q", a=2)), 16)

                    def chain_part1(h, qt):
                        hs = h % 2
                        hq = h % 3
                        L = (qt + 1) * P
                        g2 = qt // 2
                        ql2 = qt % 2
                        gb = hs * 2 + g2 % 2
                        if qt == 0:
                            scv, dvv = qkv_ready[h]
                            te.wait_ge(SCa.h, scv)
                            te.wait_ge(DVa.h, dvv)
                        small = (qt <= 7)
                        off = hs * 1024 if small else 0
                        if small:
                            if qt == 0:
                                if last_scp[0] > 0:
                                    te.wait_ge(DVa.h, last_scp[0])
                            else:
                                te.wait_ge(SCa.h, exp_v[(h, qt - 1)])
                        else:
                            if last_scp[0] > 0:
                                te.wait_ge(DVa.h, last_scp[0])
                            if qt == 8:
                                for hh in (2 * (h // 2), 2 * (h // 2) + 1):
                                    if (hh, 7) in exp_v:
                                        te.wait_ge(SCa.h, exp_v[(hh, 7)])
                        nc512 = (L + 511) // 512
                        for c in range(nc512):
                            n0 = c * 512
                            n1 = min(L, n0 + 512)
                            lastc = (c == nc512 - 1)
                            te.matmul(ps_s[:, off + n0:off + n1],
                                      Qh[hq][:, qt * P:(qt + 1) * P],
                                      Kh[hq][:, n0:n1], start=True, stop=not lastc)
                            if lastc:
                                i1 = te.matmul(ps_s[:, off + qt * P:off + L],
                                               ident[:], cmask16[:],
                                               start=False, stop=True)
                        sc_pe = PEa.inc(i1)
                        if qt > 0:
                            emit_trev(h, qt - 1)
                            if qt >= 2 and qt % 2 == 0:
                                emit_av(h, qt - 1)
                        v.wait_ge(PEa.h, sc_pe)
                        key = (h, qt - 1)
                        if not small:
                            if key in exp_v:
                                v.wait_ge(SCa.h, exp_v[key])
                            if key in trpe_v and trpe_v[key] is not None:
                                v.wait_ge(PEa.h, trpe_v[key])
                            last_scp[0] = DVa.inc(v.tensor_copy(
                                ssb[hs][:, 0:L], ps_s[:, 0:L]))
                            srow = ssb[hs][:, 0:L]
                        else:
                            srow = ps_s[:, off:off + L]
                        srow_v[(h, qt)] = srow
                        mx_dv = DVa.inc(v.reduce_max(nbx[:, hs:hs + 1], srow,
                                                     axis=AX.X, negate=True))
                        mx_v[(h, qt)] = mx_dv
                        if ql2 == 0 and qt < NT - 1:
                            mkey = (h, g2 - 2) if g2 >= 2 else (h - 2, g2 + 6)
                            if mkey in av_pe:
                                v.wait_ge(PEa.h, av_pe[mkey])
                            et3 = ETg[gb][:].rearrange("p (a n) -> p a n", a=16)
                            mz_v[(h, g2)] = DVa.inc(v.memset(
                                et3[:, qt + 1, 0:P], 0.0))

                    def chain_part2(h, qt):
                        hs = h % 2
                        L = (qt + 1) * P
                        sc.wait_ge(DVa.h, mx_v[(h, qt)])
                        exp_v[(h, qt)] = SCa.inc(sc.activation(
                            E1b[hs][:, 0:L], srow_v[(h, qt)], AF.Exp,
                            bias=nbx[:, hs:hs + 1],
                            accum_out=den[hs][:, qt:qt + 1]))
                        v.wait_ge(SCa.h, exp_v[(h, qt)])
                        if (h, qt - 1) in trpe_v and trpe_v[(h, qt - 1)] is not None:
                            v.wait_ge(PEa.h, trpe_v[(h, qt - 1)])
                        v.tensor_scalar(denb[:], den[hs][:], 1e-8, None, ALU.add)
                        v.reciprocal(rdenb[:], denb[:])
                        e2_v[(h, qt)] = DVa.inc(v.tensor_scalar(
                            E2b[hs][:, 0:L], E1b[hs][:, 0:L], rdenb[:, qt:qt + 1],
                            None, ALU.mult))

                    trpe_v = {}
                    mx_v = {}
                    srow_v = {}
                    av_free = [0]
                    emit_qkv(0)
                    emit_qkv(1)
                    npair = attn_heads // 2
                    for pr in range(npair):
                        hA, hB = 2 * pr, 2 * pr + 1
                        pend = []
                        pendB = []
                        if pr + 1 < npair:
                            g.wait_ge(PEa.h, PEa.v)
                            awload(g, hA + 2, 0)
                            aw_v.append(0)
                            awload(g, hB + 2, 1)
                            aw_v.append(0)
                            aw_v[hA + 2] = AW.v - 48
                            aw_v[hB + 2] = AW.v
                            pend = qkv_units(hA + 2)
                            pendB = qkv_units(hB + 2)
                        for qt in range(NT):
                            chain_part1(hA, qt)
                            chain_part1(hB, qt)
                            if qt >= 4:
                                for _ in range(2):
                                    if pend:
                                        pend.pop(0)()
                            chain_part2(hA, qt)
                            chain_part2(hB, qt)
                        emit_trev(hA, NT - 1)
                        emit_av(hA, NT - 1)
                        while pend:
                            pend.pop(0)()
                        emit_trev(hB, NT - 1)
                        emit_av(hB, NT - 1)
                        if pr + 1 < npair:
                            for u in pendB:
                                u()

                run_walks(walk_heads, [AW, PEa, SCa, DVa, STa], "heads")
                heads_toks = [(PEa, PEa.v), (DVa, DVa.v), (SCa, SCa.v), (STa, STa.v),
                              (AW, AW.v), (lnsems["LX"], lnsems["LX"].v)]

            with contextlib.ExitStack() as scox:
                Eo = scox.enter_context
                woT16 = Eo(nc.sbuf_tensor(f"{tag}_woT", [P, NJ * H], BF16))
                at_sb = [Eo(nc.sbuf_tensor(f"{tag}_at{i}", [P, NH * P], BF16)) for i in range(3)]
                xnw = [Eo(nc.sbuf_tensor(f"{tag}_xnw{i}", [P, T], F32)) for i in range(2)]
                ps_o = [Eo(nc.psum_tensor(f"{tag}_pso{i}", [P, H], F32)) for i in range(2)]

                WO = mks(f"{tag}_wo")
                LA = mks(f"{tag}_la")
                LXo = mks(f"{tag}_lxo")
                PEo = mks(f"{tag}_peo")
                DVo = mks(f"{tag}_dvo")
                SXo = mks(f"{tag}_sxo")

                def walk_o(EM):
                    te = EM["tensor"]
                    v = EM["vector"]
                    sy = EM["sync"]
                    sq = EM["scalar"]
                    g = EM["gpsimd"]
                    la_v, lxo_v, peo_v, dvo_v, sxo_v = [], [], [], [], []
                    wait_all(g, heads_toks)
                    wo3 = woT16[:].rearrange("p (j n) -> p j n", j=NJ)
                    for hhalf in range(2):
                        WO.inc(g.dma_start(
                            out=wo3[:, hhalf * 8:(hhalf + 1) * 8, :],
                            in_=woT[hhalf * 1024:(hhalf + 1) * 1024, :]
                            .rearrange("(j p) n -> j p n", p=P)
                            .rearrange("j p n -> p j n")), 16)
                    wo_done = WO.v
                    for qt in range(NT):
                        s = qt % 2
                        sa = qt % 3
                        if qt == 0:
                            wait_all(sy, heads_toks)
                        if qt >= 3:
                            sy.wait_ge(PEo.h, peo_v[qt - 3])
                        la_v.append(LA.inc(sy.dma_start(
                            out=at_sb[sa][:].rearrange("p (a n) -> p a n", a=NH),
                            in_=atd[qt]), 16))
                        if qt == 0:
                            wait_all(sq, heads_toks)
                        if qt >= 2:
                            sq.wait_ge(DVo.h, dvo_v[qt - 2])
                        lxo_v.append(LXo.inc(sq.dma_start(
                            out=xta[s][:], in_=x_src[qt * P:(qt + 1) * P, :]), 16))

                        if qt == 0:
                            wait_all(te, heads_toks)
                        te.wait_ge(WO.h, wo_done)
                        te.wait_ge(LA.h, la_v[qt])
                        if qt >= 2:
                            te.wait_ge(DVo.h, dvo_v[qt - 2])
                        for j in range(NJ):
                            for cc in range(4):
                                i1 = te.matmul(ps_o[s][:, cc * 512:(cc + 1) * 512],
                                               at_sb[sa][:, j * P:(j + 1) * P],
                                               wo3[:, j, cc * 512:(cc + 1) * 512],
                                               start=(j == 0), stop=(j == NJ - 1))
                        peo_v.append(PEo.inc(i1))

                        v.wait_ge(PEo.h, peo_v[qt])
                        v.wait_ge(LXo.h, lxo_v[qt])
                        if qt >= 2:
                            v.wait_ge(SXo.h, sxo_v[qt - 2])
                        dvo_v.append(DVo.inc(v.tensor_tensor(xnw[s][:], ps_o[s][:],
                                                             xta[s][:], ALU.add)))
                        sq.wait_ge(DVo.h, dvo_v[qt])
                        sxo_v.append(SXo.inc(sq.dma_start(
                            out=x_dst[qt * P:(qt + 1) * P, :], in_=xnw[s][:]), 16))

                run_walks(walk_o, [WO, LA, LXo, PEo, DVo, SXo], "oproj")
                toks_out = [(PEo, PEo.v), (DVo, DVo.v), (SXo, SXo.v), (WO, WO.v),
                            (LA, LA.v), (LXo, LXo.v)]
            return toks_out

        last = emit_ffn("f1", x0, ffn1_dst, tok0)
        if n_phases >= 2:
            last = emit_attn("at", x1, attn_dst, last)
        if n_phases >= 3:
            last = emit_ffn("f2", x2, y, last)

        def _fin(sy):
            wait_all(sy, last)
        block.sync(_fin)

    return nc


def _perm():
    idx = np.empty(FI, dtype=np.int64)
    p = 0
    for w in range(NW):
        for r in range(4):
            for g in range(128):
                idx[p] = 4 * (128 * w + g) + r
                p += 1
    return idx


def kernel(L0_attn_x, L0_attn_wq, L0_attn_wk, L0_attn_wv, L0_attn_wo,
           L0_ffn_w_gate, L0_ffn_w_up, L0_ffn_w_down):
    perm = _perm()
    wgPh = np.ascontiguousarray(np.asarray(L0_ffn_w_gate)[:, perm])
    wuPh = np.ascontiguousarray(np.asarray(L0_ffn_w_up)[:, perm])
    wdPh = np.ascontiguousarray(np.asarray(L0_ffn_w_down)[perm, :])
    wqTh = np.ascontiguousarray(np.asarray(L0_attn_wq).T)
    wkTh = np.ascontiguousarray(np.asarray(L0_attn_wk).T)
    wvTh = np.ascontiguousarray(np.asarray(L0_attn_wv).T)
    woTh = np.ascontiguousarray(np.asarray(L0_attn_wo).T)
    x = np.ascontiguousarray(np.asarray(L0_attn_x), dtype=np.float32)

    nc = build_nc(3)
    in_maps = []
    for b in range(8):
        in_maps.append({
            "x0": x[b], "wgP": wgPh, "wuP": wuPh, "wdP": wdPh,
            "wqT": wqTh, "wkT": wkTh, "wvT": wvTh, "woT": woTh,
        })
    res = run_bass_kernel_spmd(nc, in_maps, core_ids=list(range(8)))
    out = np.stack([np.asarray(r["y"]) for r in res.results], axis=0)
    return out.astype(np.float32)


# version 15
# speedup vs baseline: 1.1662x; 1.1353x over previous
import contextlib
import numpy as np
import ml_dtypes
import concourse.bass as bass
import concourse.mybir as mybir
from concourse.bass_utils import run_bass_kernel_spmd

F32 = mybir.dt.float32
I32 = mybir.dt.int32
BF16 = mybir.dt.bfloat16
AF = mybir.ActivationFunctionType
ALU = mybir.AluOpType
AX = mybir.AxisListType

P = 128
T = 2048
H = 2048
FI = 8192
NT = T // P
NJ = H // P
NW = FI // 512
NIC = FI // P
NH = 16
DH = 128
QSCALE = 1.0 / float(np.sqrt(DH))
EPS = 1e-5
NEGBIG = -1.0e30
ENGS = ("gpsimd", "sync", "tensor", "scalar", "vector")


class _DryInst:
    def then_inc(self, *a, **k):
        return self


_DRYINST = _DryInst()


class _DryEngine:
    def __getattr__(self, name):
        def f(*a, **k):
            return _DRYINST
        return f


_DRY = _DryEngine()


class Sem:
    __slots__ = ("h", "v", "name")

    def __init__(self, nc, ctx, name):
        self.h = ctx.enter_context(nc.semaphore(name))
        self.v = 0
        self.name = name

    def inc(self, inst, n=1):
        self.v += n
        if inst is not _DRYINST:
            inst.then_inc(self.h, n)
        return self.v

    def tok(self):
        return (self, self.v)


def wait_all(eng, toks):
    for (s, v) in toks:
        if v > 0:
            eng.wait_ge(s.h, v)


def build_nc(n_phases=3, attn_heads=NH):
    nc = bass.Bass("TRN2", target_bir_lowering=False)
    ap = lambda t: t.ap()

    x0 = ap(nc.dram_tensor("x0", [T, H], F32, kind="ExternalInput"))
    wgP = ap(nc.dram_tensor("wgP", [H, FI], F32, kind="ExternalInput"))
    wuP = ap(nc.dram_tensor("wuP", [H, FI], F32, kind="ExternalInput"))
    wdP = ap(nc.dram_tensor("wdP", [FI, H], F32, kind="ExternalInput"))
    wqT = ap(nc.dram_tensor("wqT", [H, H], F32, kind="ExternalInput"))
    wkT = ap(nc.dram_tensor("wkT", [H, H], F32, kind="ExternalInput"))
    wvT = ap(nc.dram_tensor("wvT", [H, H], F32, kind="ExternalInput"))
    woT = ap(nc.dram_tensor("woT", [H, H], F32, kind="ExternalInput"))
    y = ap(nc.dram_tensor("y", [T, H], F32, kind="ExternalOutput"))
    x1 = ap(nc.dram_tensor("x1", [T, H], F32))
    x2 = ap(nc.dram_tensor("x2", [T, H], F32))
    hT = ap(nc.dram_tensor("hT", [NT, P, NIC, P], BF16))
    atd = ap(nc.dram_tensor("atd", [NT, P, NH, P], BF16))

    ffn1_dst = y if n_phases == 1 else x1
    attn_dst = y if n_phases == 2 else x2

    ctx = contextlib.ExitStack()
    with ctx:
        E = ctx.enter_context
        ident = E(nc.sbuf_tensor("ident", [P, P], BF16))
        cmask32 = E(nc.sbuf_tensor("cmask32", [P, P], F32))
        cmask16 = E(nc.sbuf_tensor("cmask16", [P, P], BF16))
        nsum = E(nc.sbuf_tensor("nsum", [P, NT], F32))
        sumsq = E(nc.sbuf_tensor("sumsq", [P, NT], F32))
        negmean = E(nc.sbuf_tensor("negmean", [P, NT], F32))
        rstd = E(nc.sbuf_tensor("rstd", [P, NT], F32))
        tB = E(nc.sbuf_tensor("tB", [P, NT], F32))
        tC = E(nc.sbuf_tensor("tC", [P, NT], F32))
        tD = E(nc.sbuf_tensor("tD", [P, NT], F32))
        xta = [E(nc.sbuf_tensor(f"xta{i}", [P, T], F32)) for i in range(2)]
        epsrow = E(nc.sbuf_tensor("epsrow", [P, 1], F32))
        negBrow = E(nc.sbuf_tensor("negBrow", [P, 1], F32))

        def mks(name):
            return Sem(nc, ctx, name)

        gp = mks("gp")
        dv0 = mks("dv0")
        block = E(nc.Block())

        def run_walks(fn, sems, scope_name):
            snap = [(s, s.v) for s in sems]
            with nc.named_scope(scope_name):
                for en in ENGS:
                    reg = getattr(block, en)

                    def body(eng, en=en):
                        for (s, v0) in snap:
                            s.v = v0
                        emap = {n: (_DRY if n != en else eng) for n in ENGS}
                        fn(emap)
                    reg(body)

        def walk_init(EM):
            g = EM["gpsimd"]
            v = EM["vector"]
            g.memset(ident[:], 0.0)
            gp.inc(g.affine_select(out=ident[:], in_=ident[:], compare_op=ALU.not_equal,
                                   fill=1.0, base=0, pattern=[[-1, P]], channel_multiplier=1))
            g.memset(cmask32[:], 0.0)
            gp.inc(g.affine_select(out=cmask32[:], in_=cmask32[:], compare_op=ALU.is_ge,
                                   fill=NEGBIG, base=0, pattern=[[-1, P]], channel_multiplier=1))
            gp.inc(g.memset(epsrow[:], EPS))
            gp.inc(g.memset(negBrow[:], -40.0))
            v.wait_ge(gp.h, 4)
            dv0.inc(v.tensor_copy(cmask16[:], cmask32[:]))

        run_walks(walk_init, [gp, dv0], "init")
        tok0 = [gp.tok(), dv0.tok()]

        def make_ln(tag, x_src, xlnT, lnscr, xln16, ps_sq, ps_trA, ps_trB, prev):
            LX = mks(f"{tag}_lx")
            LV = mks(f"{tag}_lv")
            LS = mks(f"{tag}_ls")
            LP = mks(f"{tag}_lp")
            LE = mks(f"{tag}_le")

            def walk(EM):
                sy = EM["sync"]
                v = EM["vector"]
                sc = EM["scalar"]
                te = EM["tensor"]
                lv_stats, ls_stats = [], []
                x3 = xlnT[:].rearrange("p (j t2) -> p j t2", j=NJ)
                for t in range(NT):
                    if t == 0:
                        wait_all(sy, prev)
                    if t >= 2:
                        sy.wait_ge(LV.h, lv_stats[t - 2])
                        sy.wait_ge(LS.h, ls_stats[t - 2])
                    LX.inc(sy.dma_start(out=xta[t % 2][:], in_=x_src[t * P:(t + 1) * P, :]), 16)
                    if t == 0:
                        wait_all(v, prev)
                        wait_all(sc, prev)
                    v.wait_ge(LX.h, 16 * (t + 1))
                    lv_stats.append(LV.inc(v.reduce_sum(nsum[:, t:t + 1], xta[t % 2][:],
                                                        axis=AX.X, negate=True)))
                    sc.wait_ge(LX.h, 16 * (t + 1))
                    ls_stats.append(LS.inc(sc.activation(ps_sq[:], xta[t % 2][:], AF.Square,
                                                         accum_out=sumsq[:, t:t + 1])))
                v.wait_ge(LS.h, ls_stats[-1])
                v_nm = LV.inc(v.tensor_scalar(negmean[:], nsum[:], 1.0 / H, None, ALU.mult))
                sc.wait_ge(LV.h, v_nm)
                s_m2 = LS.inc(sc.activation(tD[:], negmean[:], AF.Square))
                v.wait_ge(LS.h, s_m2)
                v_var = LV.inc(v.scalar_tensor_tensor(tC[:], sumsq[:], 1.0 / H, tD[:],
                                                      ALU.mult, ALU.subtract))
                sc.wait_ge(LV.h, v_var)
                s_sqrt = LS.inc(sc.activation(tB[:], tC[:], AF.Sqrt, bias=epsrow[:, 0:1]))
                v.wait_ge(LS.h, s_sqrt)
                v_rstd = LV.inc(v.reciprocal(rstd[:], tB[:]))
                lv_apply, lp_tr, le_ev = [], [], []
                for t in range(NT):
                    if t < 2:
                        sy.wait_ge(LV.h, v_rstd)
                    else:
                        sy.wait_ge(LV.h, lv_apply[t - 2])
                    LX.inc(sy.dma_start(out=xta[t % 2][:], in_=x_src[t * P:(t + 1) * P, :]), 16)
                    s = t % 2
                    v.wait_ge(LX.h, 16 * (NT + t + 1))
                    if t == 0:
                        v.wait_ge(LV.h, v_rstd)
                    v.tensor_scalar(lnscr[s][:], xta[s][:], negmean[:, t:t + 1], None, ALU.add)
                    lv_apply.append(LV.inc(v.tensor_scalar(xln16[s][:], lnscr[s][:],
                                                           rstd[:, t:t + 1], None, ALU.mult)))
                    te.wait_ge(LV.h, lv_apply[t])
                    if t == 0:
                        te.wait_ge(gp.h, 1)
                    if t >= 1:
                        te.wait_ge(LE.h, le_ev[t - 1][0])
                    for j in range(8):
                        i1 = te.transpose(ps_trA[:, j * P:(j + 1) * P],
                                          xln16[s][:, j * P:(j + 1) * P], ident[:])
                    vA = LP.inc(i1)
                    if t >= 1:
                        te.wait_ge(LE.h, le_ev[t - 1][1])
                    for j in range(8):
                        i2 = te.transpose(ps_trB[:, j * P:(j + 1) * P],
                                          xln16[s][:, (8 + j) * P:(9 + j) * P], ident[:])
                    vB = LP.inc(i2)
                    lp_tr.append((vA, vB))
                    v.wait_ge(LP.h, lp_tr[t][0])
                    eA = LE.inc(v.tensor_copy(x3[:, 0:8, t * P:(t + 1) * P], ps_trA[:]))
                    v.wait_ge(LP.h, lp_tr[t][1])
                    eB = LE.inc(v.tensor_copy(x3[:, 8:16, t * P:(t + 1) * P], ps_trB[:]))
                    le_ev.append((eA, eB))

            return walk, {"LX": LX, "LV": LV, "LS": LS, "LP": LP, "LE": LE}

        def emit_ffn(tag, x_src, x_dst, prev):
            with contextlib.ExitStack() as sc1x:
                E1 = sc1x.enter_context
                xlnT = E1(nc.sbuf_tensor(f"{tag}_xlnT", [P, NJ * T], BF16))
                wg16 = [E1(nc.sbuf_tensor(f"{tag}_wg{i}", [P, NJ * 512], BF16)) for i in range(2)]
                wu16 = [E1(nc.sbuf_tensor(f"{tag}_wu{i}", [P, NJ * 512], BF16)) for i in range(2)]
                lnscr = [E1(nc.sbuf_tensor(f"{tag}_lnscr{i}", [P, T], F32)) for i in range(2)]
                xln16 = [E1(nc.sbuf_tensor(f"{tag}_xln16{i}", [P, T], BF16)) for i in range(2)]
                hbuf = [E1(nc.sbuf_tensor(f"{tag}_h{i}", [P, 512], F32)) for i in range(3)]
                sg = [E1(nc.sbuf_tensor(f"{tag}_sg{i}", [P, 512], F32)) for i in range(3)]
                habs = [E1(nc.sbuf_tensor(f"{tag}_habs{i}", [P, 512], F32)) for i in range(3)]
                trb = [E1(nc.sbuf_tensor(f"{tag}_tr{i}", [P, P], F32)) for i in range(7)]
                t2b = [E1(nc.sbuf_tensor(f"{tag}_t2{i}", [P, P], F32)) for i in range(3)]
                mkb = [E1(nc.sbuf_tensor(f"{tag}_mk{i}", [P, P], F32)) for i in range(3)]
                hsp = [E1(nc.sbuf_tensor(f"{tag}_hsp{i}", [P, 512], BF16)) for i in range(3)]
                hstg = [E1(nc.sbuf_tensor(f"{tag}_hstg{i}", [P, 512], BF16)) for i in range(3)]
                ps_trA = E1(nc.psum_tensor(f"{tag}_ptA", [P, 1024], BF16))
                ps_trB = E1(nc.psum_tensor(f"{tag}_ptB", [P, 1024], BF16))

                WL = mks(f"{tag}_wl")

                def wload(g, w, buf):
                    for (dst, src) in ((wg16[buf], wgP), (wu16[buf], wuP)):
                        WL.inc(g.dma_start(
                            out=dst[:].rearrange("p (j n) -> p j n", j=NJ),
                            in_=src[:, w * 512:(w + 1) * 512]
                            .rearrange("(j p) n -> j p n", p=P)
                            .rearrange("j p n -> p j n")), 16)

                with contextlib.ExitStack() as lnps:
                    ps_sq = lnps.enter_context(nc.psum_tensor(f"{tag}_pssq", [P, H], F32))
                    lnwalk, lnsems = make_ln(tag + "ln", x_src, xlnT, lnscr, xln16,
                                             ps_sq, ps_trA, ps_trB, prev)

                    def walk_pre(EM):
                        g = EM["gpsimd"]
                        wait_all(g, prev)
                        wload(g, 0, 0)
                        wload(g, 1, 1)
                    run_walks(walk_pre, [WL], tag + "pre")
                    run_walks(lnwalk, list(lnsems.values()), tag + "ln")
                LE = lnsems["LE"]
                le_final = LE.v

                ps_g = [E1(nc.psum_tensor(f"{tag}_psg{i}", [P, 512], F32)) for i in range(3)]
                ps_u = [E1(nc.psum_tensor(f"{tag}_psu{i}", [P, 512], F32)) for i in range(3)]

                PE1 = mks(f"{tag}_pe1")
                SC1 = mks(f"{tag}_sc1")
                DV1 = mks(f"{tag}_dv1")
                SH = mks(f"{tag}_sh")
                K = NW * NT

                def walk_s1(EM):
                    te = EM["tensor"]
                    sc = EM["scalar"]
                    v = EM["vector"]
                    sy = EM["sync"]
                    g = EM["gpsimd"]
                    x3 = xlnT[:].rearrange("p (j t2) -> p j t2", j=NJ)
                    mmg, trg = [], []
                    silu_v, sh_v = [], []
                    mult_v, apply_v, evac_v = [], [], []

                    def emit_tr(k):
                        s = k % 3
                        te.wait_ge(DV1.h, apply_v[k])
                        if k >= 2:
                            te.wait_ge(DV1.h, evac_v[k - 2])
                        pt = ps_trA if k % 2 == 0 else ps_trB
                        for c in range(4):
                            i1 = te.transpose(pt[:, c * P:(c + 1) * P],
                                              hsp[s][:, c * P:(c + 1) * P], ident[:])
                        trg.append(PE1.inc(i1))

                    def emit_ev_store(kk):
                        v.wait_ge(PE1.h, trg[kk])
                        if kk >= 3:
                            v.wait_ge(SH.h, sh_v[kk - 3])
                        pt = ps_trA if kk % 2 == 0 else ps_trB
                        evac_v.append(DV1.inc(v.tensor_copy(hstg[kk % 3][:], pt[:, 0:512])))
                        ww, tt = divmod(kk, NT)
                        sy.wait_ge(DV1.h, evac_v[kk])
                        sh_v.append(SH.inc(sy.dma_start(
                            out=hT[tt][:, 4 * ww:4 * (ww + 1), :],
                            in_=hstg[kk % 3][:].rearrange("p (a t2) -> p a t2", a=4)), 16))

                    for k in range(K):
                        w, t = divmod(k, NT)
                        s = k % 3
                        if k == 0:
                            te.wait_ge(LE.h, le_final)
                        te.wait_ge(WL.h, 32 * (w + 1))
                        if k >= 3:
                            te.wait_ge(SC1.h, silu_v[k - 3])
                            te.wait_ge(DV1.h, mult_v[k - 3])
                        wg3 = wg16[w % 2][:].rearrange("p (j n) -> p j n", j=NJ)
                        wu3 = wu16[w % 2][:].rearrange("p (j n) -> p j n", j=NJ)
                        for j in range(NJ):
                            lh = x3[:, j, t * P:(t + 1) * P]
                            te.matmul(ps_g[s][:], lh, wg3[:, j, :],
                                      start=(j == 0), stop=(j == NJ - 1))
                            i2 = te.matmul(ps_u[s][:], lh, wu3[:, j, :],
                                           start=(j == 0), stop=(j == NJ - 1))
                        mmg.append(PE1.inc(i2))
                        if k >= 1:
                            emit_tr(k - 1)

                        if t == NT - 1 and w + 2 < NW:
                            g.wait_ge(PE1.h, mmg[-1])
                            wload(g, w + 2, w % 2)

                        sc.wait_ge(PE1.h, mmg[k])
                        silu_v.append(SC1.inc(sc.activation(sg[s][:], ps_g[s][:], AF.Silu)))

                        v.wait_ge(SC1.h, silu_v[k])
                        mult_v.append(DV1.inc(v.tensor_tensor(hbuf[s][:], sg[s][:],
                                                              ps_u[s][:], ALU.mult)))
                        v.tensor_scalar(habs[s][:].bitcast(I32), hbuf[s][:].bitcast(I32),
                                        0x7FFFFFFF, None, ALU.bitwise_and)
                        q = habs[s][:].rearrange("p (r g) -> p r g", r=4)
                        hq = hbuf[s][:].rearrange("p (r g) -> p r g", r=4)
                        ho = hsp[s][:].rearrange("p (r g) -> p r g", r=4)
                        v.tensor_tensor(trb[0][:], q[:, 0, :], q[:, 1, :], ALU.max)
                        v.tensor_tensor(trb[1][:], q[:, 0, :], q[:, 1, :], ALU.min)
                        v.tensor_tensor(trb[2][:], q[:, 2, :], q[:, 3, :], ALU.max)
                        v.tensor_tensor(trb[3][:], q[:, 2, :], q[:, 3, :], ALU.min)
                        v.tensor_tensor(trb[4][:], trb[0][:], trb[2][:], ALU.min)
                        v.tensor_tensor(trb[5][:], trb[1][:], trb[3][:], ALU.max)
                        v.tensor_tensor(t2b[s][:], trb[4][:], trb[5][:], ALU.max)
                        last = None
                        for r in range(4):
                            v.tensor_tensor(mkb[s][:], q[:, r, :], t2b[s][:], ALU.is_ge)
                            last = v.tensor_tensor(ho[:, r, :], hq[:, r, :], mkb[s][:],
                                                   ALU.mult)
                        apply_v.append(DV1.inc(last))
                        if k >= 1:
                            emit_ev_store(k - 1)
                    emit_tr(K - 1)
                    emit_ev_store(K - 1)

                run_walks(walk_s1, [WL, PE1, SC1, DV1, SH], tag + "_s1")
                s1_toks = [(PE1, PE1.v), (DV1, DV1.v), (SC1, SC1.v), (SH, SH.v),
                           (WL, WL.v), (lnsems["LX"], lnsems["LX"].v),
                           (lnsems["LS"], lnsems["LS"].v)]

            with contextlib.ExitStack() as sc2x:
                E2 = sc2x.enter_context
                wd16 = [E2(nc.sbuf_tensor(f"{tag}_wd{i}", [P, NIC * 512], BF16)) for i in range(2)]
                hTt = [E2(nc.sbuf_tensor(f"{tag}_hTt{i}", [P, NIC * P], BF16)) for i in range(3)]
                xoq = [E2(nc.sbuf_tensor(f"{tag}_xoq{i}", [P, 512], F32)) for i in range(2)]
                xnq = [E2(nc.sbuf_tensor(f"{tag}_xnq{i}", [P, 512], F32)) for i in range(2)]
                ps_d = [E2(nc.psum_tensor(f"{tag}_psd{i}", [P, 512], F32)) for i in range(2)]

                W2 = mks(f"{tag}_w2")
                LH = mks(f"{tag}_lh")
                LX2 = mks(f"{tag}_lx2")
                PE2 = mks(f"{tag}_pe2")
                DV2 = mks(f"{tag}_dv2")
                SX = mks(f"{tag}_sx")

                def walk_s2(EM):
                    te = EM["tensor"]
                    v = EM["vector"]
                    sy = EM["sync"]
                    sq = EM["scalar"]
                    g = EM["gpsimd"]
                    w2_v, lh_v, lx2_v, pe2_v, dv2_v, sx_v = [], [], [], [], [], []

                    def wdload(Q):
                        w2_v.append(W2.inc(g.dma_start(
                            out=wd16[Q % 2][:].rearrange("p (a n) -> p a n", a=NIC),
                            in_=wdP[:, Q * 512:(Q + 1) * 512]
                            .rearrange("(a p) n -> a p n", p=P)
                            .rearrange("a p n -> p a n")), 16))

                    wait_all(g, s1_toks)
                    wdload(0)
                    wdload(1)
                    for Q in range(4):
                        for t in range(NT):
                            k = Q * NT + t
                            s = k % 2
                            sh = k % 3
                            if k == 0:
                                wait_all(sy, s1_toks)
                            if k >= 3:
                                sy.wait_ge(PE2.h, pe2_v[k - 3])
                            lh_v.append(LH.inc(sy.dma_start(
                                out=hTt[sh][:].rearrange("p (a t2) -> p a t2", a=NIC),
                                in_=hT[t]), 16))
                            if k == 0:
                                wait_all(sq, s1_toks)
                            if k >= 2:
                                sq.wait_ge(DV2.h, dv2_v[k - 2])
                            lx2_v.append(LX2.inc(sq.dma_start(
                                out=xoq[s][:],
                                in_=x_src[t * P:(t + 1) * P, Q * 512:(Q + 1) * 512]), 16))

                            if k == 0:
                                wait_all(te, s1_toks)
                            te.wait_ge(W2.h, w2_v[Q])
                            te.wait_ge(LH.h, lh_v[k])
                            if k >= 2:
                                te.wait_ge(DV2.h, dv2_v[k - 2])
                            wd3 = wd16[Q % 2][:].rearrange("p (a n) -> p a n", a=NIC)
                            hT3 = hTt[sh][:].rearrange("p (a t2) -> p a t2", a=NIC)
                            for ic in range(NIC):
                                i1 = te.matmul(ps_d[s][:], hT3[:, ic, :], wd3[:, ic, :],
                                               start=(ic == 0), stop=(ic == NIC - 1))
                            pe2_v.append(PE2.inc(i1))

                            if t == NT - 1 and Q + 2 < 4:
                                g.wait_ge(PE2.h, pe2_v[-1])
                                wdload(Q + 2)

                            v.wait_ge(PE2.h, pe2_v[k])
                            v.wait_ge(LX2.h, lx2_v[k])
                            if k >= 2:
                                v.wait_ge(SX.h, sx_v[k - 2])
                            dv2_v.append(DV2.inc(v.scalar_tensor_tensor(
                                xnq[s][:], ps_d[s][:], 0.5, xoq[s][:], ALU.mult, ALU.add)))

                            sq.wait_ge(DV2.h, dv2_v[k])
                            sx_v.append(SX.inc(sq.dma_start(
                                out=x_dst[t * P:(t + 1) * P, Q * 512:(Q + 1) * 512],
                                in_=xnq[s][:]), 16))

                run_walks(walk_s2, [W2, LH, LX2, PE2, DV2, SX], tag + "_s2")
                toks_out = [(PE2, PE2.v), (DV2, DV2.v), (SX, SX.v), (W2, W2.v),
                            (LH, LH.v), (LX2, LX2.v)]
            return toks_out

        def emit_attn(tag, x_src, x_dst, prev):
          with contextlib.ExitStack() as scouter:
            xlnT = scouter.enter_context(nc.sbuf_tensor(f"{tag}_xlnT", [P, NJ * T], BF16))
            if True:
                pass
            with contextlib.ExitStack() as scax:
                Ea = scax.enter_context
                wq16 = [Ea(nc.sbuf_tensor(f"{tag}_wq{i}", [P, NJ * P], BF16)) for i in range(2)]
                wk16 = [Ea(nc.sbuf_tensor(f"{tag}_wk{i}", [P, NJ * P], BF16)) for i in range(2)]
                wv16 = [Ea(nc.sbuf_tensor(f"{tag}_wv{i}", [P, NJ * P], BF16)) for i in range(2)]
                Qh = [Ea(nc.sbuf_tensor(f"{tag}_Qh{i}", [P, T], BF16)) for i in range(2)]
                Kh = [Ea(nc.sbuf_tensor(f"{tag}_Kh{i}", [P, T], BF16)) for i in range(2)]
                Vtm = [Ea(nc.sbuf_tensor(f"{tag}_Vtm{i}", [P, T], BF16)) for i in range(2)]
                Qh.append(xta[0][:].bitcast(BF16)[:, 0:T])
                Kh.append(xta[0][:].bitcast(BF16)[:, T:2 * T])
                Vtm.append(xta[1][:].bitcast(BF16)[:, 0:T])
                Vfm = Ea(nc.sbuf_tensor(f"{tag}_Vfm", [P, T], BF16))
                E1b = [Ea(nc.sbuf_tensor(f"{tag}_E1{i}", [P, T], BF16)) for i in range(2)]
                ETg = [Ea(nc.sbuf_tensor(f"{tag}_ET{i}", [P, 16 * 256], BF16)) for i in range(4)]
                den = [Ea(nc.sbuf_tensor(f"{tag}_den{i}", [P, P], F32)) for i in range(2)]
                denb = Ea(nc.sbuf_tensor(f"{tag}_denb", [P, P], F32))
                rdenb = Ea(nc.sbuf_tensor(f"{tag}_rdenb", [P, P], F32))
                nbx = Ea(nc.sbuf_tensor(f"{tag}_nbx", [P, 4], F32))
                ssb = [Ea(nc.sbuf_tensor(f"{tag}_ssb{i}", [P, T], F32)) for i in range(2)]
                E2b = [ssb[i][:].bitcast(BF16)[:, 0:T] for i in range(2)]
                avstg = [Ea(nc.sbuf_tensor(f"{tag}_avs{i}", [P, 256], BF16)) for i in range(2)]
                lnscr = ssb
                xln16 = [Ea(nc.sbuf_tensor(f"{tag}_xln16{i}", [P, T], BF16)) for i in range(2)]
                ps_s = Ea(nc.psum_tensor(f"{tag}_pss", [P, H], F32))
                ps_q = [Ea(nc.psum_tensor(f"{tag}_psq{i}", [P, 512], F32)) for i in range(2)]
                ps_tr = Ea(nc.psum_tensor(f"{tag}_pstr", [P, 1024], BF16))

                AW = mks(f"{tag}_aw")

                def awload(g, h, buf):
                    for (dst, src) in ((wq16[buf], wqT), (wk16[buf], wkT), (wv16[buf], wvT)):
                        AW.inc(g.dma_start(
                            out=dst[:].rearrange("p (j n) -> p j n", j=NJ),
                            in_=src[:, h * P:(h + 1) * P]
                            .rearrange("(j p) n -> j p n", p=P)
                            .rearrange("j p n -> p j n")), 16)

                with contextlib.ExitStack() as lnps:
                    ps_trB2 = lnps.enter_context(nc.psum_tensor(f"{tag}_ptB2", [P, 1024], BF16))
                    lnwalk, lnsems = make_ln(tag + "ln", x_src, xlnT, lnscr, xln16,
                                             ps_s, ps_tr, ps_trB2, prev)

                    def walk_pre(EM):
                        g = EM["gpsimd"]
                        wait_all(g, prev)
                        awload(g, 0, 0)
                        awload(g, 1, 1)
                    run_walks(walk_pre, [AW], tag + "pre")
                    run_walks(lnwalk, list(lnsems.values()), tag + "ln")
                LE = lnsems["LE"]
                le_final = LE.v

                ps_av = Ea(nc.psum_tensor(f"{tag}_psav", [P, 512], F32))

                PEa = mks(f"{tag}_pea")
                SCa = mks(f"{tag}_sca")
                DVa = mks(f"{tag}_dva")
                STa = mks(f"{tag}_sta")
                WO = mks(f"{tag}_wo")

                def walk_heads(EM):
                    te = EM["tensor"]
                    sc = EM["scalar"]
                    v = EM["vector"]
                    sy = EM["sync"]
                    g = EM["gpsimd"]
                    x3 = xlnT[:].rearrange("p (j t2) -> p j t2", j=NJ)
                    aw_v = [AW.v - 48, AW.v]
                    psq_hist = []
                    qkv_ready = {}
                    exp_v, e2_v, etg_v, scp_v = {}, {}, {}, {}
                    av_pe, av_ev, sta_v = {}, {}, {}
                    mz_v = {}
                    trfree_v = [0]
                    sc_v = {}
                    last_exp = [0]

                    def qkv_units(h):
                        hq = h % 3
                        units = []
                        for (widx, is_q, is_v) in ((0, True, False), (1, False, False),
                                                   (2, False, True)):
                            for tc in range(4):
                                def u1(widx=widx, tc=tc, h=h):
                                    wbuf = (wq16, wk16, wv16)[widx][h % 2]
                                    w3 = wbuf[:].rearrange("p (j n) -> p j n", j=NJ)
                                    ci = len(psq_hist)
                                    if h == 0 and widx == 0 and tc == 0:
                                        te.wait_ge(LE.h, le_final)
                                    te.wait_ge(AW.h, aw_v[h])
                                    if ci >= 2:
                                        kind, val = psq_hist[ci - 2]
                                        te.wait_ge(SCa.h if kind == "s" else DVa.h, val)
                                    for j in range(NJ // 2):
                                        te.matmul(ps_q[tc % 2][:], w3[:, j, :],
                                                  x3[:, j, tc * 512:(tc + 1) * 512],
                                                  start=(j == 0), stop=False)
                                def u2(widx=widx, tc=tc, is_q=is_q, is_v=is_v, h=h, hq=hq):
                                    wbuf = (wq16, wk16, wv16)[widx][h % 2]
                                    w3 = wbuf[:].rearrange("p (j n) -> p j n", j=NJ)
                                    for j in range(NJ // 2, NJ):
                                        i1 = te.matmul(ps_q[tc % 2][:], w3[:, j, :],
                                                       x3[:, j, tc * 512:(tc + 1) * 512],
                                                       start=False, stop=(j == NJ - 1))
                                    pev = PEa.inc(i1)
                                    if is_q:
                                        sc.wait_ge(PEa.h, pev)
                                        psq_hist.append(("s", SCa.inc(sc.activation(
                                            Qh[hq][:, tc * 512:(tc + 1) * 512],
                                            ps_q[tc % 2][:], AF.Copy, scale=QSCALE))))
                                    else:
                                        tgt = Vfm if is_v else Kh[hq]
                                        v.wait_ge(PEa.h, pev)
                                        psq_hist.append(("v", DVa.inc(v.tensor_copy(
                                            tgt[:, tc * 512:(tc + 1) * 512],
                                            ps_q[tc % 2][:]))))
                                units.append(u1)
                                units.append(u2)
                        for b2 in range(2):
                            def u(b2=b2, h=h, hq=hq):
                                vfm_done = psq_hist[-1][1]
                                te.wait_ge(DVa.h, max(vfm_done, trfree_v[0]) if b2 == 0
                                           else DVa.v)
                                for j in range(8):
                                    i1 = te.transpose(
                                        ps_tr[:, j * P:(j + 1) * P],
                                        Vfm[:, (b2 * 8 + j) * P:(b2 * 8 + j + 1) * P],
                                        ident[:])
                                pev = PEa.inc(i1)
                                v.wait_ge(PEa.h, pev)
                                trfree_v[0] = DVa.inc(v.tensor_copy(
                                    Vtm[hq][:, b2 * 1024:(b2 + 1) * 1024], ps_tr[:]))
                                if b2 == 1:
                                    qkv_ready[h] = (SCa.v, DVa.v)
                            units.append(u)
                        return units

                    def emit_qkv(h):
                        for u in qkv_units(h):
                            u()

                    def emit_trev(h, qt):
                        hs = h % 2
                        g2 = qt // 2
                        ql2 = qt % 2
                        gb = hs * 2 + g2 % 2
                        mz_dv = mz_v.get((h, g2), 0)
                        nbt = qt + 1
                        trpe = None
                        for b in range((nbt + 7) // 8):
                            j0 = b * 8
                            j1 = min(nbt, j0 + 8)
                            te.wait_ge(DVa.h, e2_v[(h, qt)])
                            te.wait_ge(DVa.h, trfree_v[0])
                            for j in range(j0, j1):
                                i1 = te.transpose(
                                    ps_tr[:, (j - j0) * P:(j - j0 + 1) * P],
                                    E2b[hs][:, j * P:(j + 1) * P], ident[:])
                            trpe = pev = PEa.inc(i1)
                            v.wait_ge(PEa.h, pev)
                            if b == 0 and mz_dv > 0:
                                v.wait_ge(DVa.h, mz_dv)
                            et3 = ETg[gb][:].rearrange("p (a n) -> p a n", a=16)
                            trfree_v[0] = DVa.inc(v.tensor_copy(
                                et3[:, j0:j1, ql2 * P:(ql2 + 1) * P],
                                ps_tr[:, 0:(j1 - j0) * P]))
                        etg_v[(h, qt)] = DVa.v
                        trpe_v[(h, qt)] = trpe

                    def emit_av(h, qt):
                        hs = h % 2
                        g2 = qt // 2
                        gb = hs * 2 + g2 % 2
                        te.wait_ge(DVa.h, etg_v[(h, qt)])
                        if av_free[0] > 0:
                            te.wait_ge(DVa.h, av_free[0])
                        et3 = ETg[gb][:].rearrange("p (a n) -> p a n", a=16)
                        nkc = 2 * (g2 + 1)
                        hq = h % 3
                        for kc in range(nkc):
                            i1 = te.matmul(ps_av[:, 0:256],
                                           Vtm[hq][:, kc * P:(kc + 1) * P],
                                           et3[:, kc, :],
                                           start=(kc == 0), stop=(kc == nkc - 1))
                        av_pe[(h, g2)] = PEa.inc(i1)
                        v.wait_ge(PEa.h, av_pe[(h, g2)])
                        skey = (h, g2 - 1)
                        if skey in sta_v:
                            v.wait_ge(STa.h, sta_v[skey])
                        av_free[0] = av_ev[(h, g2)] = DVa.inc(v.tensor_copy(
                            avstg[hs][:], ps_av[:, 0:256]))
                        sy.wait_ge(DVa.h, av_ev[(h, g2)])
                        sta_v[(h, g2)] = STa.inc(sy.dma_start(
                            out=atd[2 * g2:2 * (g2 + 1), :, h, :]
                            .rearrange("a d q -> d a q"),
                            in_=avstg[hs][:]
                            .rearrange("p (a q) -> p a q", a=2)), 16)

                    def chain_part1(h, qt):
                        hs = h % 2
                        hq = h % 3
                        L = (qt + 1) * P
                        g2 = qt // 2
                        ql2 = qt % 2
                        gb = hs * 2 + g2 % 2
                        if qt == 0:
                            scv, dvv = qkv_ready[h]
                            te.wait_ge(SCa.h, scv)
                            te.wait_ge(DVa.h, dvv)
                        if last_exp[0] > 0:
                            te.wait_ge(SCa.h, last_exp[0])
                        nc512 = (L + 511) // 512
                        for c in range(nc512):
                            n0 = c * 512
                            n1 = min(L, n0 + 512)
                            lastc = (c == nc512 - 1)
                            te.matmul(ps_s[:, n0:n1],
                                      Qh[hq][:, qt * P:(qt + 1) * P],
                                      Kh[hq][:, n0:n1], start=True, stop=not lastc)
                            if lastc:
                                i1 = te.matmul(ps_s[:, qt * P:L], ident[:], cmask16[:],
                                               start=False, stop=True)
                        sc_v[(h, qt)] = PEa.inc(i1)
                        if ql2 == 0 and qt < NT - 1:
                            mkey = (h, g2 - 2) if g2 >= 2 else (h - 2, g2 + 6)
                            if mkey in av_pe:
                                v.wait_ge(PEa.h, av_pe[mkey])
                            et3 = ETg[gb][:].rearrange("p (a n) -> p a n", a=16)
                            mz_v[(h, g2)] = DVa.inc(v.memset(
                                et3[:, qt + 1, 0:P], 0.0))

                    def chain_part2a(h, qt):
                        hs = h % 2
                        L = (qt + 1) * P
                        ekey = (h, qt - 1) if qt > 0 else (h - 2, NT - 1)
                        if ekey in e2_v:
                            sc.wait_ge(DVa.h, e2_v[ekey])
                        sc.wait_ge(PEa.h, sc_v[(h, qt)])
                        last_exp[0] = exp_v[(h, qt)] = SCa.inc(sc.activation(
                            E1b[hs][:, 0:L], ps_s[:, 0:L], AF.Exp,
                            bias=negBrow[:, 0:1],
                            accum_out=den[hs][:, qt:qt + 1]))

                    def chain_part2b(h, qt):
                        hs = h % 2
                        L = (qt + 1) * P
                        sc.activation(nbx[:, hs:hs + 1], den[hs][:, qt:qt + 1], AF.Ln)
                        rd_sc = SCa.inc(sc.activation(nbx[:, 2 + hs:3 + hs],
                                                      nbx[:, hs:hs + 1], AF.Exp,
                                                      scale=-1.0))
                        v.wait_ge(SCa.h, rd_sc)
                        if (h, qt - 1) in trpe_v and trpe_v[(h, qt - 1)] is not None:
                            v.wait_ge(PEa.h, trpe_v[(h, qt - 1)])
                        e2_v[(h, qt)] = DVa.inc(v.tensor_scalar(
                            E2b[hs][:, 0:L], E1b[hs][:, 0:L], nbx[:, 2 + hs:3 + hs],
                            None, ALU.mult))

                    trpe_v = {}
                    mx_v = {}
                    av_free = [0]
                    emit_qkv(0)
                    emit_qkv(1)
                    npair = attn_heads // 2
                    for pr in range(npair):
                        hA, hB = 2 * pr, 2 * pr + 1
                        pend = []
                        pendB = []
                        if pr + 1 < npair:
                            g.wait_ge(PEa.h, PEa.v)
                            awload(g, hA + 2, 0)
                            aw_v.append(0)
                            awload(g, hB + 2, 1)
                            aw_v.append(0)
                            aw_v[hA + 2] = AW.v - 48
                            aw_v[hB + 2] = AW.v
                            pend = qkv_units(hA + 2)
                            pendB = qkv_units(hB + 2)
                        def drain1():
                            if pend:
                                pend.pop(0)()
                        for qt in range(NT):
                            chain_part1(hA, qt)
                            chain_part2a(hA, qt)
                            if qt % 2 == 1 and qt >= 3:
                                emit_av(hB, qt - 2)
                            if qt >= 1:
                                drain1()
                            if qt > 0:
                                emit_trev(hA, qt - 1)
                            chain_part1(hB, qt)
                            if qt % 2 == 0 and qt >= 2:
                                emit_av(hA, qt - 1)
                            chain_part2a(hB, qt)
                            if qt >= 1:
                                drain1()
                            if qt > 0:
                                emit_trev(hB, qt - 1)
                            chain_part2b(hA, qt)
                            chain_part2b(hB, qt)
                        emit_trev(hA, NT - 1)
                        emit_av(hA, NT - 1)
                        while pend:
                            pend.pop(0)()
                        emit_trev(hB, NT - 1)
                        emit_av(hB, NT - 1)
                        for u in pendB:
                            u()

                run_walks(walk_heads, [AW, PEa, SCa, DVa, STa, WO], "heads")
                heads_toks = [(PEa, PEa.v), (DVa, DVa.v), (SCa, SCa.v), (STa, STa.v),
                              (AW, AW.v), (lnsems["LX"], lnsems["LX"].v)]

            with contextlib.ExitStack() as scox:
                Eo = scox.enter_context
                woT16 = xlnT
                at_sb = [Eo(nc.sbuf_tensor(f"{tag}_at{i}", [P, NH * P], BF16)) for i in range(3)]
                xnw = [Eo(nc.sbuf_tensor(f"{tag}_xnw{i}", [P, T], F32)) for i in range(2)]
                ps_o = [Eo(nc.psum_tensor(f"{tag}_pso{i}", [P, H], F32)) for i in range(2)]

                LA = mks(f"{tag}_la")
                LXo = mks(f"{tag}_lxo")
                PEo = mks(f"{tag}_peo")
                DVo = mks(f"{tag}_dvo")
                SXo = mks(f"{tag}_sxo")

                def walk_o(EM):
                    te = EM["tensor"]
                    v = EM["vector"]
                    sy = EM["sync"]
                    sq = EM["scalar"]
                    g = EM["gpsimd"]
                    la_v, lxo_v, peo_v, dvo_v, sxo_v = [], [], [], [], []
                    wait_all(g, heads_toks)
                    wo3 = woT16[:].rearrange("p (j n) -> p j n", j=NJ)
                    for hhalf in range(2):
                        WO.inc(g.dma_start(
                            out=wo3[:, hhalf * 8:(hhalf + 1) * 8, :],
                            in_=woT[hhalf * 1024:(hhalf + 1) * 1024, :]
                            .rearrange("(j p) n -> j p n", p=P)
                            .rearrange("j p n -> p j n")), 16)
                    wo_done = WO.v
                    for qt in range(NT):
                        s = qt % 2
                        sa = qt % 3
                        if qt == 0:
                            wait_all(sy, heads_toks)
                        if qt >= 3:
                            sy.wait_ge(PEo.h, peo_v[qt - 3])
                        la_v.append(LA.inc(sy.dma_start(
                            out=at_sb[sa][:].rearrange("p (a n) -> p a n", a=NH),
                            in_=atd[qt]), 16))
                        if qt == 0:
                            wait_all(sq, heads_toks)
                        if qt >= 2:
                            sq.wait_ge(DVo.h, dvo_v[qt - 2])
                        lxo_v.append(LXo.inc(sq.dma_start(
                            out=xta[s][:], in_=x_src[qt * P:(qt + 1) * P, :]), 16))

                        if qt == 0:
                            wait_all(te, heads_toks)
                        te.wait_ge(WO.h, wo_done)
                        te.wait_ge(LA.h, la_v[qt])
                        if qt >= 2:
                            te.wait_ge(DVo.h, dvo_v[qt - 2])
                        for j in range(NJ):
                            for cc in range(4):
                                i1 = te.matmul(ps_o[s][:, cc * 512:(cc + 1) * 512],
                                               at_sb[sa][:, j * P:(j + 1) * P],
                                               wo3[:, j, cc * 512:(cc + 1) * 512],
                                               start=(j == 0), stop=(j == NJ - 1))
                        peo_v.append(PEo.inc(i1))

                        v.wait_ge(PEo.h, peo_v[qt])
                        v.wait_ge(LXo.h, lxo_v[qt])
                        if qt >= 2:
                            v.wait_ge(SXo.h, sxo_v[qt - 2])
                        dvo_v.append(DVo.inc(v.tensor_tensor(xnw[s][:], ps_o[s][:],
                                                             xta[s][:], ALU.add)))
                        sq.wait_ge(DVo.h, dvo_v[qt])
                        sxo_v.append(SXo.inc(sq.dma_start(
                            out=x_dst[qt * P:(qt + 1) * P, :], in_=xnw[s][:]), 16))

                run_walks(walk_o, [WO, LA, LXo, PEo, DVo, SXo], "oproj")
                toks_out = [(PEo, PEo.v), (DVo, DVo.v), (SXo, SXo.v), (WO, WO.v),
                            (LA, LA.v), (LXo, LXo.v)]
            return toks_out

        last = emit_ffn("f1", x0, ffn1_dst, tok0)
        if n_phases >= 2:
            last = emit_attn("at", x1, attn_dst, last)
        if n_phases >= 3:
            last = emit_ffn("f2", x2, y, last)

        def _fin(sy):
            wait_all(sy, last)
        block.sync(_fin)

    return nc


def _perm():
    idx = np.empty(FI, dtype=np.int64)
    p = 0
    for w in range(NW):
        for r in range(4):
            for g in range(128):
                idx[p] = 4 * (128 * w + g) + r
                p += 1
    return idx


def kernel(L0_attn_x, L0_attn_wq, L0_attn_wk, L0_attn_wv, L0_attn_wo,
           L0_ffn_w_gate, L0_ffn_w_up, L0_ffn_w_down):
    perm = _perm()
    wgPh = np.ascontiguousarray(np.asarray(L0_ffn_w_gate)[:, perm])
    wuPh = np.ascontiguousarray(np.asarray(L0_ffn_w_up)[:, perm])
    wdPh = np.ascontiguousarray(np.asarray(L0_ffn_w_down)[perm, :])
    wqTh = np.ascontiguousarray(np.asarray(L0_attn_wq).T)
    wkTh = np.ascontiguousarray(np.asarray(L0_attn_wk).T)
    wvTh = np.ascontiguousarray(np.asarray(L0_attn_wv).T)
    woTh = np.ascontiguousarray(np.asarray(L0_attn_wo).T)
    x = np.ascontiguousarray(np.asarray(L0_attn_x), dtype=np.float32)

    nc = build_nc(3)
    in_maps = []
    for b in range(8):
        in_maps.append({
            "x0": x[b], "wgP": wgPh, "wuP": wuPh, "wdP": wdPh,
            "wqT": wqTh, "wkT": wkTh, "wvT": wvTh, "woT": woTh,
        })
    res = run_bass_kernel_spmd(nc, in_maps, core_ids=list(range(8)))
    out = np.stack([np.asarray(r["y"]) for r in res.results], axis=0)
    return out.astype(np.float32)
